# version 02
# baseline (speedup 1.0000x reference)
import numpy as np
from contextlib import ExitStack

import concourse.bass as bass
import concourse.mybir as mybir
from concourse.bass_utils import run_bass_kernel_spmd

F32 = mybir.dt.float32
BF16 = mybir.dt.bfloat16
AF = mybir.ActivationFunctionType
ALU = mybir.AluOpType
AXX = mybir.AxisListType.X

S = 2048
DM = 1024
HH = 2048
NE = 32
NI = 512
DC = 8
HCN = 16
NT = 4
SH = S // 2
NTH = 2
LN_EPS = 1e-5
NORM_EPS = 1e-8


class _Res:
    __slots__ = ("w", "r")

    def __init__(self):
        self.w = {}
        self.r = {}


def build(debug=False):
    nc = bass.Bass("TRN2", target_bir_lowering=False)

    x_d = nc.dram_tensor("x", [S, DM], F32, kind="ExternalInput").ap()
    rms_d = nc.dram_tensor("rms", [1, DM], F32, kind="ExternalInput").ap()
    spl_d = nc.dram_tensor("spl", [1, DM], F32, kind="ExternalInput").ap()
    wk_d = nc.dram_tensor("wk", [DM, HH], F32, kind="ExternalInput").ap()
    wugg_d = nc.dram_tensor("wugg", [DM, 3 * HH], F32, kind="ExternalInput").ap()
    wout_d = nc.dram_tensor("wout", [HH, DM], F32, kind="ExternalInput").ap()
    gw_d = nc.dram_tensor("gw", [NE, DM], F32, kind="ExternalInput").ap()
    gb_d = nc.dram_tensor("gb", [1, NE], F32, kind="ExternalInput").ap()
    eg_d = nc.dram_tensor("eg", [NE, NI, DM], F32, kind="ExternalInput").ap()
    eu_d = nc.dram_tensor("eu", [NE, NI, DM], F32, kind="ExternalInput").ap()
    ed_d = nc.dram_tensor("ed", [NE, DM, NI], F32, kind="ExternalInput").ap()
    shg_d = nc.dram_tensor("shg", [DM, DM], F32, kind="ExternalInput").ap()
    shu_d = nc.dram_tensor("shu", [DM, DM], F32, kind="ExternalInput").ap()
    shd_d = nc.dram_tensor("shd", [DM, DM], F32, kind="ExternalInput").ap()
    out_d = nc.dram_tensor("out", [S, DM], F32, kind="ExternalOutput").ap()

    eg_st = nc.dram_tensor("eg_st", [NE, NI, DM], BF16, kind="Internal").ap()
    eu_st = nc.dram_tensor("eu_st", [NE, NI, DM], BF16, kind="Internal").ap()
    ed_st = nc.dram_tensor("ed_st", [NE, DM, NI], BF16, kind="Internal").ap()
    shg_st = nc.dram_tensor("shg_st", [DM, DM], BF16, kind="Internal").ap()
    shu_st = nc.dram_tensor("shu_st", [DM, DM], BF16, kind="Internal").ap()
    shd_st = nc.dram_tensor("shd_st", [DM, DM], BF16, kind="Internal").ap()

    dbg = {}
    if debug:
        dbg["xn"] = nc.dram_tensor("d_xn", [128, DC, S], BF16, kind="ExternalOutput").ap()
        dbg["gated"] = nc.dram_tensor("d_gated", [128, NT, HCN * 512], BF16, kind="ExternalOutput").ap()
        dbg["strm"] = nc.dram_tensor("d_strm", [128, NT, DC * 512], F32, kind="ExternalOutput").ap()
        dbg["ln2"] = nc.dram_tensor("d_ln2", [2, 128, DC, SH], BF16, kind="ExternalOutput").ap()
        dbg["lgf"] = nc.dram_tensor("d_lgf", [2, NE, SH], F32, kind="ExternalOutput").ap()
        dbg["wfm"] = nc.dram_tensor("d_wfm", [2, NE, SH], F32, kind="ExternalOutput").ap()
        dbg["routed"] = nc.dram_tensor("d_routed", [2, 128, DC, SH], F32, kind="ExternalOutput").ap()

    es = ExitStack()

    sems = {}
    cnt = {}
    for name in ("ct", "cv", "cs", "cg", "syc", "sx0", "sx1", "sgw0", "sgw1", "sgo",
                 "ss", "ss2", "st0", "st1", "st2", "st3", "so", "sdbg"):
        sems[name] = es.enter_context(nc.semaphore(name))
        cnt[name] = 0
    wm = {e: {} for e in ("t", "v", "s", "g", "y")}

    import collections
    res = collections.defaultdict(_Res)

    def wt(eng, en, deps):
        coll = {}
        for name, val in deps:
            if not val:
                continue
            if coll.get(name, 0) < val:
                coll[name] = val
        for name, val in coll.items():
            if wm[en].get(name, 0) >= val:
                continue
            wm[en][name] = val
            eng.wait_ge(sems[name], val)

    def pre(eng, en, reads=(), writes=()):
        deps = []
        for b in reads:
            deps.extend(res[b].w.items())
        for b in writes:
            deps.extend(res[b].w.items())
            deps.extend(res[b].r.items())
        wt(eng, en, deps)

    def post(sem, val, reads=(), writes=()):
        for b in reads:
            r = res[b].r
            if r.get(sem, 0) < val:
                r[sem] = val
        for b in writes:
            w = res[b].w
            if w.get(sem, 0) < val:
                w[sem] = val

    def inc(instr, name, k=1):
        cnt[name] += k
        instr.then_inc(sems[name], k)
        return cnt[name]

    def barrier():
        deps = [("ct", cnt["ct"]), ("cv", cnt["cv"]), ("cs", cnt["cs"]), ("cg", cnt["cg"]),
                ("sdbg", cnt["sdbg"]), ("so", cnt["so"])]

        @block.tensor
        def _(t):
            wt(t, "t", deps)

        @block.vector
        def _(v):
            wt(v, "v", deps)

        @block.scalar
        def _(s):
            wt(s, "s", deps)

        @block.gpsimd
        def _(g):
            wt(g, "g", deps)

        @block.sync
        def _(y):
            wt(y, "y", deps)

    identf = es.enter_context(nc.sbuf_tensor("identf", [128, 128], F32))
    identb = es.enter_context(nc.sbuf_tensor("identb", [128, 128], BF16))
    ones128 = es.enter_context(nc.sbuf_tensor("ones128", [128, 1], F32))
    onesc = es.enter_context(nc.sbuf_tensor("onesc", [1, 128], F32))
    iok = es.enter_context(nc.sbuf_tensor("iok", [NE, 128], F32))
    sel_ee = es.enter_context(nc.sbuf_tensor("sel_ee", [NE, 128], F32))
    gwT = es.enter_context(nc.sbuf_tensor("gwT", [128, DC, NE], F32))
    gbb = es.enter_context(nc.sbuf_tensor("gbb", [1, NE], F32))
    bias_rep = es.enter_context(nc.sbuf_tensor("bias_rep", [128, NE], F32))
    murow = es.enter_context(nc.sbuf_tensor("murow", [1, 512], F32))
    rrow = es.enter_context(nc.sbuf_tensor("rrow", [1, 512], F32))
    rowsc = es.enter_context(nc.sbuf_tensor("rowsc", [1, 512 * 3], F32))
    arena = es.enter_context(nc.sbuf_tensor("arena", [128, NT, DC * 512], F32))
    arena_b = arena[:, :, :].bitcast(BF16)

    def gated_ap(hc, tt=None):
        if tt is None:
            return arena_b[:, :, hc * 512:(hc + 1) * 512]
        return arena_b[:, tt, hc * 512:(hc + 1) * 512]

    def strm_ap(dc, gtt):
        return arena[:, gtt, dc * 512:(dc + 1) * 512]

    AR = ["ar0", "ar1", "ar2", "ar3"]
    C = [rowsc[0:1, 512 * i:512 * (i + 1)] for i in range(3)]

    def row_finalize(s1ap, s2ap, bank1, bank2, n):
        @block.vector
        def _(v):
            pre(v, "v", reads=(bank1, bank2), writes=("rows",))
            i0 = inc(v.tensor_scalar(murow[:, :], s1ap, 1.0 / n, None, ALU.mult), "cv")
            wt(v, "v", [("cv", i0)])
            i1 = inc(v.tensor_scalar(C[0], s2ap, 1.0 / n, None, ALU.mult), "cv")
            wt(v, "v", [("cv", i1)])
            i2 = inc(v.tensor_tensor(C[1], murow[:, :], murow[:, :], ALU.mult), "cv")
            wt(v, "v", [("cv", i2)])
            i3 = inc(v.tensor_tensor(C[1], C[0], C[1], ALU.subtract), "cv")
            wt(v, "v", [("cv", i3)])
            i4 = inc(v.tensor_scalar(C[0], C[1], LN_EPS, None, ALU.add), "cv")
            post("cv", i4, reads=(bank1, bank2), writes=("rows",))
        rv1 = cnt["cv"]

        @block.scalar
        def _(s):
            wt(s, "s", [("cv", rv1)])
            pre(s, "s", writes=("rows2",))
            i5 = inc(s.activation(C[2], C[0], AF.Sqrt), "cs")
            post("cs", i5, writes=("rows2",))
        rs1 = cnt["cs"]

        @block.vector
        def _(v):
            wt(v, "v", [("cs", rs1)])
            i6 = inc(v.reciprocal(C[1], C[2]), "cv")
            wt(v, "v", [("cv", i6)])
            i7 = inc(v.tensor_tensor(C[2], C[0], C[1], ALU.mult), "cv")
            wt(v, "v", [("cv", i7)])
            i8 = inc(v.tensor_tensor(C[2], C[2], C[1], ALU.mult), "cv")
            wt(v, "v", [("cv", i8)])
            i9 = inc(v.tensor_scalar(C[2], C[2], -0.5, 1.5, ALU.mult, ALU.add), "cv")
            wt(v, "v", [("cv", i9)])
            i10 = inc(v.tensor_tensor(rrow[:, :], C[1], C[2], ALU.mult), "cv")
            post("cv", i10, writes=("rows",))
        return cnt["cv"]

    swout = ExitStack()
    wout_sb = swout.enter_context(nc.sbuf_tensor("wout_sb", [128, HCN, DM], BF16))
    sxn = ExitStack()
    xn_fm = sxn.enter_context(nc.sbuf_tensor("xn_fm", [128, DC, S], BF16))
    sio = ExitStack()
    io1 = sio.enter_context(nc.sbuf_tensor("io1", [128, 128], F32))
    io2 = sio.enter_context(nc.sbuf_tensor("io2", [128, 128], F32))
    gwb = sio.enter_context(nc.sbuf_tensor("gwb", [NE, DM], F32))

    with nc.Block() as block:
        @block.gpsimd
        def _(g):
            i1 = inc(g.iota(io1[:, :], [[1, 128]], channel_multiplier=0,
                            allow_small_or_imprecise_dtypes=True), "cg")
            post("cg", i1, writes=("io1",))
            i2 = inc(g.iota(io2[:, :], [[0, 128]], channel_multiplier=1,
                            allow_small_or_imprecise_dtypes=True), "cg")
            post("cg", i2, writes=("io2",))
            i3 = inc(g.iota(iok[:, :], [[0, 128]], channel_multiplier=1,
                            allow_small_or_imprecise_dtypes=True), "cg")
            post("cg", i3, writes=("iok",))

        @block.sync
        def _(y):
            inc(y.dma_start(gwb[:, :], gw_d), "syc", 16)
            inc(y.dma_start(gbb[:, :], gb_d), "syc", 16)

        @block.vector
        def _(v):
            pre(v, "v", reads=("io1", "io2"), writes=("identf",))
            i1 = inc(v.tensor_tensor(identf[:, :], io1[:, :], io2[:, :], ALU.is_equal), "cv")
            post("cv", i1, reads=("io1", "io2"), writes=("identf",))
            wt(v, "v", [("cv", i1)])
            i2 = inc(v.tensor_copy(identb[:, :], identf[:, :]), "cv")
            post("cv", i2, reads=("identf",), writes=("identb",))
            i3 = inc(v.memset(ones128[:, :], 1.0), "cv")
            post("cv", i3, writes=("ones128",))
            i4 = inc(v.memset(onesc[:, :], 1.0), "cv")
            post("cv", i4, writes=("onesc",))
        ST_SH = 16 * 3
        ST_EXP = [16 * (3 + 3 * (e + 1)) for e in range(NE)]

        with ExitStack() as s0:
            pk0 = s0.enter_context(nc.psum_tensor("pk0", [128, 512], F32))
            pk1 = s0.enter_context(nc.psum_tensor("pk1", [128, 512], F32))
            pbb = s0.enter_context(nc.psum_tensor("pbb", [128, 2048], BF16))
            rmsb = s0.enter_context(nc.sbuf_tensor("rmsb", [1, DM], F32))
            splb = s0.enter_context(nc.sbuf_tensor("splb", [1, DM], F32))
            crow = s0.enter_context(nc.sbuf_tensor("crow", [1, DM], F32))
            c_rep = s0.enter_context(nc.sbuf_tensor("c_rep", [128, DM], F32))
            x4 = [s0.enter_context(nc.sbuf_tensor(f"x4_{i}", [128, 4, DM], F32)) for i in range(2)]
            sqd = s0.enter_context(nc.sbuf_tensor("sqd_a0", [128, DM], F32))
            xn_tmp = s0.enter_context(nc.sbuf_tensor("xn_tmp", [128, DM], BF16))
            xn1 = s0.enter_context(nc.sbuf_tensor("xn1_a0", [128, DM], F32))
            sums = s0.enter_context(nc.sbuf_tensor("sums", [128, 16], F32))
            sumsq = s0.enter_context(nc.sbuf_tensor("sumsq", [128, 16], F32))
            mu_a = s0.enter_context(nc.sbuf_tensor("mu_a", [128, 16], F32))
            ap_a = s0.enter_context(nc.sbuf_tensor("ap_a", [128, 16], F32))
            g_sc = s0.enter_context(nc.sbuf_tensor("g_sc", [128, 8 * 16], F32))
            GC = [g_sc[:, 16 * i:16 * (i + 1)] for i in range(8)]

            @block.sync
            def _(y):
                inc(y.dma_start(rmsb[:, :], rms_d), "syc", 16)
                post("syc", inc(y.dma_start(splb[:, :], spl_d), "syc", 16), writes=("splb",))

            @block.vector
            def _(v):
                pre(v, "v", reads=("rmsb", "splb"), writes=("crow",))
                i1 = inc(v.tensor_tensor(crow[:, :], rmsb[:, :], splb[:, :], ALU.mult), "cv")
                post("cv", i1, reads=("rmsb", "splb"), writes=("crow",))

            @block.tensor
            def _(t):
                pre(t, "t", reads=("onesc", "crow"), writes=("pk0", "pk1"))
                i0 = inc(t.matmul(pk0[:, :], onesc[:, :], crow[0:1, 0:512], start=True, stop=True), "ct")
                i1 = inc(t.matmul(pk1[:, :], onesc[:, :], crow[0:1, 512:1024], start=True, stop=True), "ct")
                post("ct", i1, reads=("onesc", "crow"), writes=("pk0", "pk1"))

            @block.scalar
            def _(s):
                pre(s, "s", reads=("pk0", "pk1"), writes=("c_rep",))
                inc(s.copy(c_rep[:, 0:512], pk0[:, :]), "cs")
                i1 = inc(s.copy(c_rep[:, 512:1024], pk1[:, :]), "cs")
                post("cs", i1, reads=("pk0", "pk1"), writes=("c_rep",))

            @block.tensor
            def _(t):
                pre(t, "t", reads=("onesc", "gbb", "gwb", "identf"), writes=("pk0", "pk1"))
                inc(t.matmul(pk0[0:128, 0:NE], onesc[:, :], gbb[:, :], start=True, stop=True), "ct")
                for c in range(DC):
                    last = t.transpose(pk1[:, c * NE:(c + 1) * NE], gwb[:, c * 128:(c + 1) * 128],
                                       identf[0:NE, 0:NE])
                i1 = inc(last, "ct")
                post("ct", i1, reads=("onesc", "gbb", "gwb", "identf"), writes=("pk0", "pk1"))

            @block.scalar
            def _(s):
                pre(s, "s", reads=("pk0", "pk1"), writes=("bias_rep", "gwT"))
                inc(s.copy(bias_rep[:, :], pk0[0:128, 0:NE]), "cs")
                i1 = inc(s.copy(gwT[:, :, :],
                                pk1[:, 0:DC * NE].rearrange("p (c e) -> p c e", e=NE)), "cs")
                post("cs", i1, reads=("pk0", "pk1"), writes=("bias_rep", "gwT"))

            for grp in range(4):
                slot = grp % 2
                xb = f"x4_{slot}"

                @block.sync
                def _(y, grp=grp, slot=slot, xb=xb):
                    pre(y, "y", writes=(xb,))
                    src = bass.AP(x_d.tensor, grp * 512 * DM, [[DM, 128], [128 * DM, 4], [1, DM]])
                    sxn_ = f"sx{slot}"
                    post(sxn_, inc(y.dma_start(x4[slot][:, :, :], src), sxn_, 16), writes=(xb,))

                for j in range(4):
                    i = grp * 4 + j

                    @block.vector
                    def _(v, i=i, j=j, slot=slot, xb=xb):
                        pre(v, "v", reads=(xb,), writes=("a0st",))
                        iv = inc(v.reduce_sum(sums[:, i:i + 1], x4[slot][:, j, :], axis=AXX), "cv")
                        post("cv", iv, reads=(xb,), writes=("a0st",))

                    @block.vector
                    def _(v, i=i, j=j, slot=slot, xb=xb):
                        pre(v, "v", reads=(xb,), writes=("sqd", "a0st"))
                        ix = inc(v.scalar_tensor_tensor(sqd[:, :], x4[slot][:, j, :], 0.0,
                                                        x4[slot][:, j, :], ALU.bypass, ALU.mult,
                                                        accum_out=sumsq[:, i:i + 1]), "cv")
                        post("cv", ix, reads=(xb,), writes=("sqd", "a0st"))

                sl = slice(grp * 4, grp * 4 + 4)

                @block.vector
                def _(v, sl=sl):
                    pre(v, "v", reads=("a0st",), writes=("a0st",))
                    i0 = inc(v.tensor_scalar(mu_a[:, sl], sums[:, sl], 1.0 / DM, None, ALU.mult), "cv")
                    wt(v, "v", [("cv", i0)])
                    i1 = inc(v.tensor_scalar(GC[0][:, sl], sumsq[:, sl], 1.0 / DM, None, ALU.mult), "cv")
                    wt(v, "v", [("cv", i1)])
                    i2 = inc(v.tensor_tensor(GC[1][:, sl], mu_a[:, sl], mu_a[:, sl], ALU.mult), "cv")
                    wt(v, "v", [("cv", i2)])
                    i3 = inc(v.tensor_tensor(GC[1][:, sl], GC[0][:, sl], GC[1][:, sl], ALU.subtract), "cv")
                    wt(v, "v", [("cv", i3)])
                    i4 = inc(v.tensor_scalar(GC[2][:, sl], GC[1][:, sl], LN_EPS, None, ALU.add), "cv")
                    post("cv", i4, writes=("a0st",))
                a0f1 = cnt["cv"]

                @block.scalar
                def _(s, sl=sl):
                    wt(s, "s", [("cv", a0f1)])
                    i5 = inc(s.activation(GC[3][:, sl], GC[2][:, sl], AF.Sqrt), "cs")
                    post("cs", i5, writes=("a0st2",))
                a0f2 = cnt["cs"]

                @block.vector
                def _(v, sl=sl):
                    wt(v, "v", [("cs", a0f2)])
                    i6 = inc(v.reciprocal(GC[4][:, sl], GC[3][:, sl]), "cv")
                    wt(v, "v", [("cv", i6)])
                    i7 = inc(v.tensor_tensor(GC[5][:, sl], GC[2][:, sl], GC[4][:, sl], ALU.mult), "cv")
                    wt(v, "v", [("cv", i7)])
                    i8 = inc(v.tensor_tensor(GC[5][:, sl], GC[5][:, sl], GC[4][:, sl], ALU.mult), "cv")
                    wt(v, "v", [("cv", i8)])
                    i9 = inc(v.tensor_scalar(GC[5][:, sl], GC[5][:, sl], -0.5, 1.5, ALU.mult, ALU.add), "cv")
                    wt(v, "v", [("cv", i9)])
                    i10 = inc(v.tensor_tensor(GC[4][:, sl], GC[4][:, sl], GC[5][:, sl], ALU.mult), "cv")
                    wt(v, "v", [("cv", i10)])
                    i11 = inc(v.tensor_tensor(GC[6][:, sl], GC[1][:, sl], GC[4][:, sl], ALU.mult), "cv")
                    wt(v, "v", [("cv", i11)])
                    i12 = inc(v.tensor_tensor(GC[6][:, sl], GC[6][:, sl], GC[4][:, sl], ALU.mult), "cv")
                    post("cv", i12, writes=("a0st",))
                a0f3 = cnt["cv"]

                @block.scalar
                def _(s, sl=sl):
                    wt(s, "s", [("cv", a0f3)])
                    i13 = inc(s.activation(GC[7][:, sl], GC[6][:, sl], AF.Sqrt), "cs")
                    post("cs", i13, writes=("a0st2",))
                a0f4 = cnt["cs"]

                @block.vector
                def _(v, sl=sl):
                    wt(v, "v", [("cs", a0f4)])
                    i14 = inc(v.tensor_scalar(GC[7][:, sl], GC[7][:, sl], 1e-20, None, ALU.max), "cv")
                    wt(v, "v", [("cv", i14)])
                    i15 = inc(v.reciprocal(GC[3][:, sl], GC[7][:, sl]), "cv")
                    wt(v, "v", [("cv", i15)])
                    i16 = inc(v.tensor_tensor(GC[3][:, sl], GC[6][:, sl], GC[3][:, sl], ALU.mult), "cv")
                    wt(v, "v", [("cv", i16)])
                    i17 = inc(v.tensor_tensor(GC[3][:, sl], GC[7][:, sl], GC[3][:, sl], ALU.add), "cv")
                    wt(v, "v", [("cv", i17)])
                    i18 = inc(v.tensor_scalar(GC[3][:, sl], GC[3][:, sl], 0.5, NORM_EPS, ALU.mult, ALU.add), "cv")
                    wt(v, "v", [("cv", i18)])
                    i19 = inc(v.reciprocal(GC[3][:, sl], GC[3][:, sl]), "cv")
                    wt(v, "v", [("cv", i19)])
                    i20 = inc(v.tensor_tensor(ap_a[:, sl], GC[4][:, sl], GC[3][:, sl], ALU.mult), "cv")
                    post("cv", i20, writes=("a0st",))

                for j in range(4):
                    i = grp * 4 + j

                    @block.vector
                    def _(v, i=i, j=j, slot=slot, xb=xb):
                        pre(v, "v", reads=(xb, "a0st", "c_rep"), writes=("xn1", "xn_tmp"))
                        iv1 = inc(v.tensor_scalar(xn1[:, :], x4[slot][:, j, :], mu_a[:, i:i + 1],
                                                  ap_a[:, i:i + 1], ALU.subtract, ALU.mult), "cv")
                        wt(v, "v", [("cv", iv1)])
                        iv2 = inc(v.tensor_tensor(xn_tmp[:, :], xn1[:, :], c_rep[:, :], ALU.mult), "cv")
                        post("cv", iv2, reads=(xb, "a0st", "c_rep"), writes=("xn1", "xn_tmp"))

                    pb = f"pbb{i % 2}"
                    half0 = (i % 2) * 1024

                    @block.tensor
                    def _(t, i=i, half0=half0, pb=pb):
                        pre(t, "t", reads=("xn_tmp", "identb"), writes=(pb,))
                        for c in range(DC):
                            last = t.transpose(pbb[:, half0 + c * 128: half0 + (c + 1) * 128],
                                               xn_tmp[:, c * 128:(c + 1) * 128], identb[:, :])
                        it = inc(last, "ct")
                        post("ct", it, reads=("xn_tmp", "identb"), writes=(pb,))

                    @block.scalar
                    def _(s, i=i, half0=half0, pb=pb):
                        pre(s, "s", reads=(pb,), writes=("xn",))
                        ic_ = inc(s.copy(xn_fm[:, :, i * 128:(i + 1) * 128],
                                         pbb[:, half0:half0 + 1024].rearrange("p (c k) -> p c k", k=128)), "cs")
                        post("cs", ic_, reads=(pb,), writes=("xn",))
                        res["xn"].w["cs"] = ic_

            if debug:
                @block.gpsimd
                def _(g):
                    pre(g, "g", reads=("xn",))
                    post("sdbg", inc(g.dma_start(dbg["xn"], xn_fm[:, :, :]), "sdbg", 16), reads=("xn",))
            barrier()

        sio.close()

        with ExitStack() as s1:
            pj = [s1.enter_context(nc.psum_tensor(f"pj{i}", [128, 512], F32)) for i in range(8)]
            wkb = [s1.enter_context(nc.sbuf_tensor(f"wkb{i}", [128, 8, 128], BF16)) for i in range(2)]
            wub = [s1.enter_context(nc.sbuf_tensor(f"wub{i}", [128, 8, 3, 128], BF16)) for i in range(2)]
            K_sb = s1.enter_context(nc.sbuf_tensor("K_sb", [128, S], F32))
            go_sb = s1.enter_context(nc.sbuf_tensor("go_sb", [128, S], F32))
            b_sb = s1.enter_context(nc.sbuf_tensor("b_sb", [128, S], F32))
            sgi = s1.enter_context(nc.sbuf_tensor("sgi", [128, 512], F32))
            kst0 = s1.enter_context(nc.sbuf_tensor("kst0", [128, 512], F32))
            t1s = [s1.enter_context(nc.sbuf_tensor(f"t1s{i}", [128, 512], F32)) for i in range(2)]

            for hc in range(HCN):
                slot = hc % 2
                wsl = f"a1w{slot}"

                @block.gpsimd
                def _(g, hc=hc, slot=slot, wsl=wsl):
                    pre(g, "g", writes=(wsl,))
                    sgn = f"sgw{slot}"
                    srcap = bass.AP(wk_d.tensor, hc * 128, [[HH, 128], [128 * HH, 8], [1, 128]])
                    inc(g.dma_start(wkb[slot][:, :, :], srcap), sgn, 16)
                    for b in range(3):
                        srcap = bass.AP(wugg_d.tensor, b * HH + hc * 128,
                                        [[3 * HH, 128], [128 * 3 * HH, 8], [1, 128]])
                        inc(g.dma_start(wub[slot][:, :, b, :], srcap), sgn, 16)
                    post(sgn, cnt[sgn], writes=(wsl,))
                    if hc == 0:
                        inc(g.dma_start(shg_st, shg_d), "ss2", 16)
                        inc(g.dma_start(shu_st, shu_d), "ss2", 16)
                        inc(g.dma_start(shd_st, shd_d), "ss2", 16)
                    for e in range(2 * hc, 2 * hc + 2):
                        inc(g.dma_start(eg_st[e], eg_d[e]), "ss", 16)
                        inc(g.dma_start(eu_st[e], eu_d[e]), "ss", 16)
                        inc(g.dma_start(ed_st[e], ed_d[e]), "ss", 16)

                if hc == 1:
                    @block.gpsimd
                    def _(g):
                        pre(g, "g", writes=("woutsb",))
                        src = bass.AP(wout_d.tensor, 0, [[DM, 128], [128 * DM, 16], [1, DM]])
                        post("sgo", inc(g.dma_start(wout_sb[:, :, :], src), "sgo", 16),
                             writes=("woutsb",))

                @block.tensor
                def _(t, hc=hc, slot=slot, wsl=wsl):
                    for tt in range(NT):
                        pre(t, "t", reads=("xn", wsl), writes=(f"pj{tt}",))
                        for kc in range(DC):
                            last = t.matmul(pj[tt][:, :], wkb[slot][:, kc, :],
                                            xn_fm[:, kc, tt * 512:(tt + 1) * 512],
                                            start=(kc == 0), stop=(kc == DC - 1))
                        it = inc(last, "ct")
                        post("ct", it, reads=("xn", wsl), writes=(f"pj{tt}",))
                    for tt in range(NT):
                        pre(t, "t", reads=("xn", wsl), writes=(f"pj{4 + tt}",))
                        for kc in range(DC):
                            last = t.matmul(pj[4 + tt][:, :], wub[slot][:, kc, 0, :],
                                            xn_fm[:, kc, tt * 512:(tt + 1) * 512],
                                            start=(kc == 0), stop=(kc == DC - 1))
                        it = inc(last, "ct")
                        post("ct", it, reads=("xn", wsl), writes=(f"pj{4 + tt}",))

                @block.scalar
                def _(s, hc=hc):
                    for tt in range(NT):
                        pre(s, "s", reads=(f"pj{tt}",), writes=("Ksb",))
                        ic_ = inc(s.activation(K_sb[:, tt * 512:(tt + 1) * 512], pj[tt][:, :],
                                               AF.Sigmoid), "cs")
                        post("cs", ic_, reads=(f"pj{tt}",), writes=("Ksb",))
                        res["Ksb"].w["cs"] = ic_

                @block.tensor
                def _(t, hc=hc, slot=slot, wsl=wsl):
                    for tt in range(NT):
                        pre(t, "t", reads=("xn", wsl), writes=(f"pj{tt}",))
                        for kc in range(DC):
                            last = t.matmul(pj[tt][:, :], wub[slot][:, kc, 1, :],
                                            xn_fm[:, kc, tt * 512:(tt + 1) * 512],
                                            start=(kc == 0), stop=(kc == DC - 1))
                        it = inc(last, "ct")
                        post("ct", it, reads=("xn", wsl), writes=(f"pj{tt}",))

                for tt in range(NT):
                    @block.scalar
                    def _(s, hc=hc, tt=tt):
                        pre(s, "s", reads=(f"pj{tt}",), writes=("sgi",))
                        ic_ = inc(s.activation(sgi[:, :], pj[tt][:, :], AF.Sigmoid), "cs")
                        post("cs", ic_, reads=(f"pj{tt}",), writes=("sgi",))

                    tb = f"t1s{tt % 2}"

                    @block.vector
                    def _(v, hc=hc, tt=tt, tb=tb):
                        pre(v, "v", reads=("sgi", f"pj{4 + tt}"), writes=(tb,))
                        iv1 = inc(v.tensor_tensor(t1s[tt % 2][:, :], pj[4 + tt][:, :], sgi[:, :],
                                                  ALU.mult), "cv")
                        post("cv", iv1, reads=("sgi", f"pj{4 + tt}"), writes=(tb,))
                        pre(v, "v", reads=(tb, "Ksb"), writes=("bsb",))
                        iv2 = inc(v.tensor_tensor(b_sb[:, tt * 512:(tt + 1) * 512], t1s[tt % 2][:, :],
                                                  K_sb[:, tt * 512:(tt + 1) * 512], ALU.mult), "cv")
                        wt(v, "v", [("cv", iv2)])
                        iv3 = inc(v.tensor_tensor(b_sb[:, tt * 512:(tt + 1) * 512], t1s[tt % 2][:, :],
                                                  b_sb[:, tt * 512:(tt + 1) * 512], ALU.subtract), "cv")
                        post("cv", iv3, reads=(tb, "Ksb"), writes=("bsb",))
                        res["bsb"].w["cv"] = iv3

                @block.tensor
                def _(t, hc=hc, slot=slot, wsl=wsl):
                    for tt in range(NT):
                        pre(t, "t", reads=("xn", wsl), writes=(f"pj{4 + tt}",))
                        for kc in range(DC):
                            last = t.matmul(pj[4 + tt][:, :], wub[slot][:, kc, 2, :],
                                            xn_fm[:, kc, tt * 512:(tt + 1) * 512],
                                            start=(kc == 0), stop=(kc == DC - 1))
                        it = inc(last, "ct")
                        post("ct", it, reads=("xn", wsl), writes=(f"pj{4 + tt}",))

                @block.scalar
                def _(s, hc=hc):
                    for tt in range(NT):
                        pre(s, "s", reads=(f"pj{4 + tt}",), writes=("gosb",))
                        ic_ = inc(s.activation(go_sb[:, tt * 512:(tt + 1) * 512], pj[4 + tt][:, :],
                                               AF.Sigmoid), "cs")
                        post("cs", ic_, reads=(f"pj{4 + tt}",), writes=("gosb",))
                        res["gosb"].w["cs"] = ic_


                @block.vector
                def _(v, hc=hc):
                    pre(v, "v", reads=("Ksb",), writes=("kst0",))
                    iv0 = inc(v.tensor_copy(kst0[:, 1:512], K_sb[:, 0:511]), "cv")
                    wt(v, "v", [("cv", iv0)])
                    iv = inc(v.memset(kst0[:, 0:1], 0.0), "cv")
                    post("cv", iv, reads=("Ksb",), writes=("kst0",))
                    for tt in range(NT):
                        pre(v, "v", reads=("kst0", "Ksb", "bsb"), writes=("bsb",))
                        if tt == 0:
                            a_ap = kst0[:, :]
                            init = 0.0
                        else:
                            a_ap = K_sb[:, tt * 512 - 1:(tt + 1) * 512 - 1]
                            init = b_sb[:, tt * 512 - 1:tt * 512]
                        iv2 = inc(v.tensor_tensor_scan(b_sb[:, tt * 512:(tt + 1) * 512], a_ap,
                                                       b_sb[:, tt * 512:(tt + 1) * 512], init,
                                                       ALU.mult, ALU.add), "cv")
                        post("cv", iv2, reads=("kst0", "Ksb"), writes=("bsb",))
                    pre(v, "v", reads=("bsb", "gosb"), writes=AR)
                    iv3 = inc(v.tensor_tensor(gated_ap(hc),
                                              b_sb[:, :].rearrange("p (tt k) -> p tt k", k=512),
                                              go_sb[:, :].rearrange("p (tt k) -> p tt k", k=512),
                                              ALU.mult), "cv")
                    post("cv", iv3, reads=("bsb", "gosb"))
                    for a in AR:
                        res[a].w["cv"] = iv3

            if debug:
                @block.gpsimd
                def _(g):
                    pre(g, "g", reads=AR)
                    post("sdbg", inc(g.dma_start(dbg["gated"], arena_b[:, :, :]), "sdbg", 16), reads=AR)
            barrier()

        sxn.close()

        with ExitStack() as s2:
            pxT = [s2.enter_context(nc.psum_tensor(f"pxT{i}", [128, 512], F32)) for i in range(2)]
            py = [s2.enter_context(nc.psum_tensor(f"py{i}", [128, 512], F32)) for i in range(2)]
            prow1 = s2.enter_context(nc.psum_tensor("prow1", [128, 512], F32))
            prow2 = s2.enter_context(nc.psum_tensor("prow2", [128, 512], F32))
            pmu = s2.enter_context(nc.psum_tensor("pmu", [128, 512], F32))
            prr = s2.enter_context(nc.psum_tensor("prr", [128, 512], F32))
            x2b = [s2.enter_context(nc.sbuf_tensor(f"x2b{i}", [128, 4, DM], F32)) for i in range(2)]
            xf_sb = s2.enter_context(nc.sbuf_tensor("xf_sb", [128, DC, 512], F32))
            bo_sb = s2.enter_context(nc.sbuf_tensor("bo_sb", [128, DC, 512], F32))
            sq2 = [s2.enter_context(nc.sbuf_tensor(f"sq2{i}", [128, 512], F32)) for i in range(2)]
            tb2 = [s2.enter_context(nc.sbuf_tensor(f"tb2{i}", [128, 512], F32)) for i in range(2)]
            mu1r = s2.enter_context(nc.sbuf_tensor("mu1r", [1, 512], F32))
            r1r = s2.enter_context(nc.sbuf_tensor("r1r", [1, 512], F32))

            for tt in range(NT):
                xs = f"x2b{tt % 2}"

                @block.sync
                def _(y, tt=tt, xs=xs):
                    pre(y, "y", writes=(xs,))
                    src = bass.AP(x_d.tensor, tt * 512 * DM, [[DM, 128], [128 * DM, 4], [1, DM]])
                    sxn_ = f"sx{tt % 2}"
                    post(sxn_, inc(y.dma_start(x2b[tt % 2][:, :, :], src), sxn_, 16), writes=(xs,))

                for dc in range(DC):
                    pxb = f"pxT{dc % 2}"

                    @block.tensor
                    def _(t, tt=tt, dc=dc, xs=xs, pxb=pxb):
                        pre(t, "t", reads=(xs, "identf"), writes=(pxb,))
                        for j in range(4):
                            last = t.transpose(pxT[dc % 2][:, j * 128:(j + 1) * 128],
                                               x2b[tt % 2][:, j, dc * 128:(dc + 1) * 128], identf[:, :])
                        it = inc(last, "ct")
                        post("ct", it, reads=(xs, "identf"), writes=(pxb,))

                    @block.scalar
                    def _(s, tt=tt, dc=dc, pxb=pxb):
                        pre(s, "s", reads=(pxb,), writes=("xf",))
                        ic_ = inc(s.copy(xf_sb[:, dc, :], pxT[dc % 2][:, :]), "cs")
                        post("cs", ic_, reads=(pxb,), writes=("xf",))
                        res["xf"].w["cs"] = ic_

                @block.tensor
                def _(t, tt=tt):
                    pre(t, "t", reads=("xf", "ones128"), writes=("prow1",))
                    for dc in range(DC):
                        last1 = t.matmul(prow1[0:1, :], ones128[:, :], xf_sb[:, dc, :],
                                         start=(dc == 0), stop=(dc == DC - 1))
                    it1 = inc(last1, "ct")
                    post("ct", it1, reads=("xf", "ones128"), writes=("prow1",))

                for dc in range(DC):
                    @block.vector
                    def _(v, tt=tt, dc=dc):
                        pre(v, "v", reads=("xf",), writes=(f"sq2{dc % 2}",))
                        ic_ = inc(v.scalar_tensor_tensor(sq2[dc % 2][:, :], xf_sb[:, dc, :], 0.0,
                                                         xf_sb[:, dc, :], ALU.bypass, ALU.mult), "cv")
                        post("cv", ic_, reads=("xf",), writes=(f"sq2{dc % 2}",))

                    @block.tensor
                    def _(t, tt=tt, dc=dc):
                        pre(t, "t", reads=(f"sq2{dc % 2}",), writes=("prow2",) if dc == 0 else ())
                        last2 = t.matmul(prow2[0:1, :], ones128[:, :], sq2[dc % 2][:, :],
                                         start=(dc == 0), stop=(dc == DC - 1))
                        it2 = inc(last2, "ct")
                        post("ct", it2, reads=(f"sq2{dc % 2}",))
                        res["prow2"].w["ct"] = cnt["ct"]

                row_finalize(prow1[0:1, :], prow2[0:1, :], "prow1", "prow2", DM)

                @block.vector
                def _(v, tt=tt):
                    pre(v, "v", reads=("rows",), writes=("mu1r",))
                    iv = inc(v.tensor_copy(mu1r[:, :], murow[:, :]), "cv")
                    wt(v, "v", [("cv", iv)])
                    iv = inc(v.tensor_copy(r1r[:, :], rrow[:, :]), "cv")
                    post("cv", iv, reads=("rows",), writes=("mu1r",))

                @block.tensor
                def _(t, tt=tt):
                    pre(t, "t", reads=("mu1r", "onesc"), writes=("pmu", "prr"))
                    inc(t.matmul(pmu[:, :], onesc[:, :], mu1r[0:1, :], start=True, stop=True), "ct")
                    it = inc(t.matmul(prr[:, :], onesc[:, :], r1r[0:1, :], start=True, stop=True), "ct")
                    post("ct", it, reads=("mu1r", "onesc"), writes=("pmu", "prr"))

                for dc in range(DC):
                    pyb = f"py{dc % 2}"

                    @block.tensor
                    def _(t, tt=tt, dc=dc, pyb=pyb):
                        pre(t, "t", reads=("woutsb", AR[tt]), writes=(pyb,))
                        for hcb in range(HCN):
                            last = t.matmul(py[dc % 2][:, :], wout_sb[:, hcb, dc * 128:(dc + 1) * 128],
                                            gated_ap(hcb, tt), start=(hcb == 0), stop=(hcb == HCN - 1))
                        it2 = inc(last, "ct")
                        post("ct", it2, reads=("woutsb", AR[tt]), writes=(pyb,))

                    @block.vector
                    def _(v, tt=tt, dc=dc, pyb=pyb):
                        tbb = f"tb2{dc % 2}"
                        pre(v, "v", reads=("xf", "pmu", "prr", pyb), writes=(tbb, "bo"))
                        iv1 = inc(v.tensor_tensor(tb2[dc % 2][:, :], xf_sb[:, dc, :], pmu[:, :],
                                                  ALU.subtract), "cv")
                        wt(v, "v", [("cv", iv1)])
                        iv2 = inc(v.tensor_tensor(tb2[dc % 2][:, :], tb2[dc % 2][:, :], prr[:, :],
                                                  ALU.mult), "cv")
                        wt(v, "v", [("cv", iv2)])
                        iv3 = inc(v.tensor_tensor(bo_sb[:, dc, :], tb2[dc % 2][:, :],
                                                  py[dc % 2][:, :], ALU.add), "cv")
                        post("cv", iv3, reads=("xf", "pmu", "prr", pyb), writes=(tbb, "bo"))

                @block.tensor
                def _(t, tt=tt):
                    pre(t, "t", reads=("bo", "ones128"), writes=("prow1",))
                    for dc in range(DC):
                        last1 = t.matmul(prow1[0:1, :], ones128[:, :], bo_sb[:, dc, :],
                                         start=(dc == 0), stop=(dc == DC - 1))
                    it1 = inc(last1, "ct")
                    post("ct", it1, reads=("bo", "ones128"), writes=("prow1",))

                for dc in range(DC):
                    @block.vector
                    def _(v, tt=tt, dc=dc):
                        pre(v, "v", reads=("bo",), writes=(f"sq2{dc % 2}",))
                        ic_ = inc(v.scalar_tensor_tensor(sq2[dc % 2][:, :], bo_sb[:, dc, :], 0.0,
                                                         bo_sb[:, dc, :], ALU.bypass, ALU.mult), "cv")
                        post("cv", ic_, reads=("bo",), writes=(f"sq2{dc % 2}",))

                    @block.tensor
                    def _(t, tt=tt, dc=dc):
                        pre(t, "t", reads=(f"sq2{dc % 2}",), writes=("prow2",) if dc == 0 else ())
                        last2 = t.matmul(prow2[0:1, :], ones128[:, :], sq2[dc % 2][:, :],
                                         start=(dc == 0), stop=(dc == DC - 1))
                        it2 = inc(last2, "ct")
                        post("ct", it2, reads=(f"sq2{dc % 2}",))
                        res["prow2"].w["ct"] = cnt["ct"]

                row_finalize(prow1[0:1, :], prow2[0:1, :], "prow1", "prow2", DM)

                @block.tensor
                def _(t, tt=tt):
                    pre(t, "t", reads=("rows", "onesc"), writes=("pmu", "prr"))
                    inc(t.matmul(pmu[:, :], onesc[:, :], murow[0:1, :], start=True, stop=True), "ct")
                    it = inc(t.matmul(prr[:, :], onesc[:, :], rrow[0:1, :], start=True, stop=True), "ct")
                    post("ct", it, reads=("rows", "onesc"), writes=("pmu", "prr"))

                @block.vector
                def _(v, tt=tt):
                    for dc in range(DC):
                        tbb = f"tb2{dc % 2}"
                        pre(v, "v", reads=("bo", "pmu", "prr", "xf"), writes=(tbb, AR[tt]))
                        iv1 = inc(v.tensor_tensor(tb2[dc % 2][:, :], bo_sb[:, dc, :], pmu[:, :],
                                                  ALU.subtract), "cv")
                        wt(v, "v", [("cv", iv1)])
                        iv2 = inc(v.tensor_tensor(tb2[dc % 2][:, :], tb2[dc % 2][:, :], prr[:, :],
                                                  ALU.mult), "cv")
                        wt(v, "v", [("cv", iv2)])
                        iv3 = inc(v.tensor_tensor(strm_ap(dc, tt), tb2[dc % 2][:, :],
                                                  xf_sb[:, dc, :], ALU.add), "cv")
                        post("cv", iv3, reads=("bo", "pmu", "prr", "xf"), writes=(tbb,))
                        res[AR[tt]].w["cv"] = iv3

            if debug:
                @block.gpsimd
                def _(g):
                    pre(g, "g", reads=AR)
                    post("sdbg", inc(g.dma_start(dbg["strm"], arena[:, :, :]), "sdbg", 16), reads=AR)
            barrier()

        swout.close()

        with ExitStack() as s3:
            pf = [s3.enter_context(nc.psum_tensor(f"pf{i}", [128, 512], F32)) for i in range(8)]
            PF = [f"pf{i}" for i in range(8)]
            ln2b = s3.enter_context(nc.sbuf_tensor("ln2b", [128, DC, SH], BF16))
            routed = s3.enter_context(nc.sbuf_tensor("routed", [128, DC, SH], F32))
            lgf = s3.enter_context(nc.sbuf_tensor("lgf", [NE, SH], F32))
            w_fm = s3.enter_context(nc.sbuf_tensor("w_fm", [NE, SH], F32))
            ln2c = [s3.enter_context(nc.sbuf_tensor(f"ln2c{i}", [128, 512], F32)) for i in range(2)]
            sqf = [s3.enter_context(nc.sbuf_tensor(f"sqf{i}", [128, 512], F32)) for i in range(2)]
            tbf = [s3.enter_context(nc.sbuf_tensor(f"tbf{i}", [128, 512], F32)) for i in range(2)]
            rt_sc = s3.enter_context(nc.sbuf_tensor("rt_sc", [128, 160], F32))
            lgtm = s3.enter_context(nc.sbuf_tensor("lgtm", [128, NE], F32))
            wdtm = s3.enter_context(nc.sbuf_tensor("wdtm", [128, NE], F32))
            sel32 = s3.enter_context(nc.sbuf_tensor("sel32", [128, NE], F32))
            wr32 = s3.enter_context(nc.sbuf_tensor("wr32", [128, NE], F32))
            sgl = s3.enter_context(nc.sbuf_tensor("sgl", [128, 512], F32))

            sco = rt_sc[:, 0:32]
            scb = rt_sc[:, 32:64]
            srt = rt_sc[:, 64:96]
            msk = rt_sc[:, 96:128]
            gsr = rt_sc[:, 128:132]
            u2r = rt_sc[:, 132:134]
            v2r = rt_sc[:, 134:136]
            m2a = rt_sc[:, 136:137]
            m2b = rt_sc[:, 137:138]
            secr = rt_sc[:, 138:139]
            gmr = rt_sc[:, 140:144]
            m8r = rt_sc[:, 144:152]
            denr = rt_sc[:, 152:153]

            for half in range(2):
                hoff = half * NTH

                for tt in range(NTH):
                    gtt = hoff + tt

                    @block.tensor
                    def _(t, tt=tt, gtt=gtt):
                        pre(t, "t", reads=(AR[gtt], "ones128"), writes=("pf0",))
                        for dc in range(DC):
                            last1 = t.matmul(pf[0][0:1, :], ones128[:, :], strm_ap(dc, gtt),
                                             start=(dc == 0), stop=(dc == DC - 1))
                        it1 = inc(last1, "ct")
                        post("ct", it1, reads=(AR[gtt], "ones128"), writes=("pf0",))

                    for dc in range(DC):
                        @block.vector
                        def _(v, tt=tt, gtt=gtt, dc=dc):
                            pre(v, "v", reads=(AR[gtt],), writes=(f"sqf{dc % 2}",))
                            ic_ = inc(v.scalar_tensor_tensor(sqf[dc % 2][:, :], strm_ap(dc, gtt), 0.0,
                                                             strm_ap(dc, gtt), ALU.bypass, ALU.mult), "cv")
                            post("cv", ic_, reads=(AR[gtt],), writes=(f"sqf{dc % 2}",))

                        @block.tensor
                        def _(t, tt=tt, dc=dc):
                            pre(t, "t", reads=(f"sqf{dc % 2}",), writes=("pf1",) if dc == 0 else ())
                            last2 = t.matmul(pf[1][0:1, :], ones128[:, :], sqf[dc % 2][:, :],
                                             start=(dc == 0), stop=(dc == DC - 1))
                            it2 = inc(last2, "ct")
                            post("ct", it2, reads=(f"sqf{dc % 2}",))
                            res["pf1"].w["ct"] = cnt["ct"]

                    row_finalize(pf[0][0:1, :], pf[1][0:1, :], "pf0", "pf1", DM)

                    @block.tensor
                    def _(t, tt=tt):
                        pre(t, "t", reads=("rows", "onesc"), writes=("pf2", "pf3"))
                        inc(t.matmul(pf[2][:, :], onesc[:, :], murow[0:1, :], start=True, stop=True), "ct")
                        it = inc(t.matmul(pf[3][:, :], onesc[:, :], rrow[0:1, :], start=True, stop=True), "ct")
                        post("ct", it, reads=("rows", "onesc"), writes=("pf2", "pf3"))

                    for dc in range(DC):
                        lcb = f"ln2c{dc % 2}"

                        @block.vector
                        def _(v, tt=tt, gtt=gtt, dc=dc, lcb=lcb):
                            pre(v, "v", reads=(AR[gtt], "pf2", "pf3"), writes=(lcb,))
                            iv1 = inc(v.tensor_tensor(ln2c[dc % 2][:, :], strm_ap(dc, gtt), pf[2][:, :],
                                                      ALU.subtract), "cv")
                            wt(v, "v", [("cv", iv1)])
                            iv2 = inc(v.tensor_tensor(ln2c[dc % 2][:, :], ln2c[dc % 2][:, :], pf[3][:, :],
                                                      ALU.mult), "cv")
                            post("cv", iv2, reads=(AR[gtt], "pf2", "pf3"), writes=(lcb,))

                        @block.scalar
                        def _(s, tt=tt, dc=dc, lcb=lcb):
                            pre(s, "s", reads=(lcb,), writes=("ln2b",))
                            ic_ = inc(s.copy(ln2b[:, dc, tt * 512:(tt + 1) * 512], ln2c[dc % 2][:, :]), "cs")
                            post("cs", ic_, reads=(lcb,), writes=("ln2b",))

                        @block.tensor
                        def _(t, tt=tt, dc=dc, lcb=lcb):
                            pre(t, "t", reads=(lcb, "gwT"), writes=("pf4",) if dc == 0 else ())
                            last = t.matmul(pf[4][0:NE, :], gwT[:, dc, :], ln2c[dc % 2][:, :],
                                            start=(dc == 0), stop=(dc == DC - 1))
                            it = inc(last, "ct")
                            post("ct", it, reads=(lcb, "gwT"))
                    res["pf4"].w["ct"] = cnt["ct"]

                    @block.vector
                    def _(v, tt=tt):
                        pre(v, "v", reads=("pf4",), writes=("lgf",))
                        ic_ = inc(v.tensor_copy(lgf[:, tt * 512:(tt + 1) * 512], pf[4][0:NE, :]), "cv")
                        post("cv", ic_, reads=("pf4",), writes=("lgf",))

                for i in range(8):
                    @block.tensor
                    def _(t, half=half, i=i):
                        pre(t, "t", reads=("lgf", "identf"), writes=("pf5",))
                        it = inc(t.transpose(pf[5][:, 0:NE], lgf[:, i * 128:(i + 1) * 128],
                                             identf[0:NE, 0:NE]), "ct")
                        post("ct", it, reads=("lgf", "identf"), writes=("pf5",))

                    @block.vector
                    def _(v, i=i):
                        pre(v, "v", reads=("pf5",), writes=("lgtm",))
                        ic1 = inc(v.tensor_copy(lgtm[:, :], pf[5][:, 0:NE]), "cv")
                        post("cv", ic1, reads=("pf5",), writes=("lgtm",))

                    @block.scalar
                    def _(s, i=i):
                        pre(s, "s", reads=("lgtm",), writes=("rtsc",))
                        ic2 = inc(s.activation(sco[:, :], lgtm[:, :], AF.Sigmoid), "cs")
                        post("cs", ic2, reads=("lgtm",), writes=("rtsc",))

                    @block.vector
                    def _(v, i=i):
                        pre(v, "v", reads=("rtsc", "bias_rep"), writes=("rtsc", "sel32v", "wr32v", "wdtm"))
                        iv = inc(v.tensor_tensor(scb[:, :], sco[:, :], bias_rep[:, :], ALU.add), "cv")
                        wt(v, "v", [("cv", iv)])
                        for g8 in range(4):
                            iv = inc(v.max(out=srt[:, 8 * g8:8 * g8 + 8], in_=scb[:, 8 * g8:8 * g8 + 8]), "cv")
                        wt(v, "v", [("cv", iv)])
                        wt(v, "v", [("cv", iv)])
                        iv = inc(v.tensor_tensor(gsr[:, :], srt[:, 0:32:8], srt[:, 1:32:8], ALU.add), "cv")
                        wt(v, "v", [("cv", iv)])
                        iv = inc(v.tensor_tensor(u2r[:, :], gsr[:, 0:4:2], gsr[:, 1:4:2], ALU.max), "cv")
                        wt(v, "v", [("cv", iv)])
                        iv = inc(v.tensor_tensor(v2r[:, :], gsr[:, 0:4:2], gsr[:, 1:4:2], ALU.min), "cv")
                        wt(v, "v", [("cv", iv)])
                        iv = inc(v.tensor_tensor(m2a[:, :], v2r[:, 0:1], v2r[:, 1:2], ALU.max), "cv")
                        wt(v, "v", [("cv", iv)])
                        iv = inc(v.tensor_tensor(m2b[:, :], u2r[:, 0:1], u2r[:, 1:2], ALU.min), "cv")
                        wt(v, "v", [("cv", iv)])
                        iv = inc(v.tensor_tensor(secr[:, :], m2a[:, :], m2b[:, :], ALU.max), "cv")
                        wt(v, "v", [("cv", iv)])
                        iv = inc(v.tensor_scalar(gmr[:, :], gsr[:, :], secr[:, 0:1], None, ALU.is_ge), "cv")
                        wt(v, "v", [("cv", iv)])
                        iv = inc(v.tensor_tensor(msk[:, :].rearrange("p (g k) -> p g k", k=8),
                                                 scb[:, :].rearrange("p (g k) -> p g k", k=8),
                                                 gmr[:, :].to_broadcast([128, 4, 8]), ALU.mult), "cv")
                        wt(v, "v", [("cv", iv)])
                        iv = inc(v.max(out=m8r[:, :], in_=msk[:, :]), "cv")
                        wt(v, "v", [("cv", iv)])
                        iv = inc(v.tensor_scalar(sel32[:, :], msk[:, :], m8r[:, 7:8], None, ALU.is_ge), "cv")
                        wt(v, "v", [("cv", iv)])
                        iv = inc(v.tensor_tensor(wr32[:, :], sco[:, :], sel32[:, :], ALU.mult), "cv")
                        wt(v, "v", [("cv", iv)])
                        iv = inc(v.reduce_sum(denr[:, :], wr32[:, :], axis=AXX), "cv")
                        wt(v, "v", [("cv", iv)])
                        iv = inc(v.tensor_scalar(denr[:, :], denr[:, :], 1e-20, None, ALU.add), "cv")
                        wt(v, "v", [("cv", iv)])
                        iv = inc(v.reciprocal(denr[:, :], denr[:, :]), "cv")
                        wt(v, "v", [("cv", iv)])
                        iv = inc(v.tensor_scalar(wdtm[:, :], wr32[:, :], denr[:, 0:1], 2.5,
                                                 ALU.mult, ALU.mult), "cv")
                        post("cv", iv, reads=("rtsc", "bias_rep"),
                             writes=("rtsc", "sel32v", "wr32v", "wdtm"))

                    @block.tensor
                    def _(t, i=i):
                        pre(t, "t", reads=("wdtm", "identf"), writes=("pf5",))
                        it = inc(t.transpose(pf[5][0:NE, 0:128], wdtm[:, :], identf[:, :]), "ct")
                        post("ct", it, reads=("wdtm", "identf"), writes=("pf5",))

                    @block.vector
                    def _(v, i=i):
                        pre(v, "v", reads=("pf5",), writes=("wfm",))
                        ic_ = inc(v.tensor_copy(w_fm[:, i * 128:(i + 1) * 128], pf[5][0:NE, 0:128]), "cv")
                        post("cv", ic_, reads=("pf5",), writes=("wfm",))

                if debug:
                    @block.gpsimd
                    def _(g, half=half):
                        pre(g, "g", reads=("lgf", "wfm", "ln2b"))
                        inc(g.dma_start(dbg["lgf"][half], lgf[:, :]), "sdbg", 16)
                        inc(g.dma_start(dbg["wfm"][half], w_fm[:, :]), "sdbg", 16)
                        ig = inc(g.dma_start(dbg["ln2"][half], ln2b[:, :, :]), "sdbg", 16)
                        post("sdbg", ig, reads=("lgf", "wfm", "ln2b"))

                sh_sc = ExitStack()
                shgT = sh_sc.enter_context(nc.sbuf_tensor(f"shgT{half}", [128, DC, 512], BF16))
                shuT = sh_sc.enter_context(nc.sbuf_tensor(f"shuT{half}", [128, DC, 512], BF16))
                shdT = sh_sc.enter_context(nc.sbuf_tensor(f"shdT{half}", [128, 4, DM], BF16))
                mid = sh_sc.enter_context(nc.sbuf_tensor(f"mid{half}", [128, 4, SH], BF16))

                for i2h in range(2):
                    @block.sync
                    def _(y, i2h=i2h):
                        wt(y, "y", [("ss2", 48)])
                        pre(y, "y", writes=("shT",))
                        inc(y.dma_start(shgT[:, :, :], shg_st[i2h * 512:(i2h + 1) * 512, :],
                                        transpose=True), "st3", 16)
                        inc(y.dma_start(shuT[:, :, :], shu_st[i2h * 512:(i2h + 1) * 512, :],
                                        transpose=True), "st3", 16)
                        inc(y.dma_start(shdT[:, :, :], shd_st[:, i2h * 512:(i2h + 1) * 512],
                                        transpose=True), "st3", 16)
                        post("st3", cnt["st3"], writes=("shT",))

                    for tt in range(NTH):
                        for ic in range(4):
                            pga, pgan = pf[0 + 2 * (ic % 2)], f"pf{0 + 2 * (ic % 2)}"
                            pua, puan = pf[1 + 2 * (ic % 2)], f"pf{1 + 2 * (ic % 2)}"

                            @block.tensor
                            def _(t, tt=tt, ic=ic, pga=pga, pua=pua, pgan=pgan, puan=puan):
                                pre(t, "t", reads=("shT", "ln2b"), writes=(pgan, puan))
                                for dc in range(DC):
                                    t.matmul(pga[:, :], shgT[:, dc, ic * 128:(ic + 1) * 128],
                                             ln2b[:, dc, tt * 512:(tt + 1) * 512],
                                             start=(dc == 0), stop=(dc == DC - 1))
                                for dc in range(DC):
                                    last = t.matmul(pua[:, :], shuT[:, dc, ic * 128:(ic + 1) * 128],
                                                    ln2b[:, dc, tt * 512:(tt + 1) * 512],
                                                    start=(dc == 0), stop=(dc == DC - 1))
                                it = inc(last, "ct")
                                post("ct", it, reads=("shT", "ln2b"), writes=(pgan, puan))

                            @block.scalar
                            def _(s, tt=tt, ic=ic, pga=pga, pgan=pgan):
                                pre(s, "s", reads=(pgan,), writes=("sgl",))
                                ic_ = inc(s.activation(sgl[:, :], pga[:, :], AF.Silu), "cs")
                                post("cs", ic_, reads=(pgan,), writes=("sgl",))

                            @block.vector
                            def _(v, tt=tt, ic=ic, pua=pua, puan=puan):
                                pre(v, "v", reads=("sgl", puan), writes=("mid",))
                                iv = inc(v.tensor_tensor(mid[:, ic, tt * 512:(tt + 1) * 512], sgl[:, :],
                                                         pua[:, :], ALU.mult), "cv")
                                post("cv", iv, reads=("sgl", puan), writes=("mid",))

                    for tt in range(NTH):
                        for dc in range(DC):
                            pda, pdan = pf[4 + dc % 2], f"pf{4 + dc % 2}"

                            @block.tensor
                            def _(t, tt=tt, dc=dc, pda=pda, pdan=pdan):
                                pre(t, "t", reads=("shT", "mid"), writes=(pdan,))
                                for ic in range(4):
                                    last = t.matmul(pda[:, :], shdT[:, ic, dc * 128:(dc + 1) * 128],
                                                    mid[:, ic, tt * 512:(tt + 1) * 512],
                                                    start=(ic == 0), stop=(ic == 3))
                                it = inc(last, "ct")
                                post("ct", it, reads=("shT", "mid"), writes=(pdan,))

                            if i2h == 0:
                                @block.vector
                                def _(v, tt=tt, dc=dc, pda=pda, pdan=pdan):
                                    pre(v, "v", reads=(pdan,), writes=("routed",))
                                    ic_ = inc(v.tensor_copy(routed[:, dc, tt * 512:(tt + 1) * 512],
                                                            pda[:, :]), "cv")
                                    post("cv", ic_, reads=(pdan,), writes=("routed",))
                            else:
                                @block.vector
                                def _(v, tt=tt, dc=dc, pda=pda, pdan=pdan):
                                    pre(v, "v", reads=(pdan, "routed"), writes=("routed",))
                                    iv = inc(v.tensor_tensor(routed[:, dc, tt * 512:(tt + 1) * 512],
                                                             routed[:, dc, tt * 512:(tt + 1) * 512],
                                                             pda[:, :], ALU.add), "cv")
                                    post("cv", iv, reads=(pdan,), writes=("routed",))

                barrier()
                sh_sc.close()
                ex_sc = ExitStack()
                egT = [ex_sc.enter_context(nc.sbuf_tensor(f"egT{half}_{i}", [128, DC, NI], BF16))
                       for i in range(2)]
                euT = [ex_sc.enter_context(nc.sbuf_tensor(f"euT{half}_{i}", [128, DC, NI], BF16))
                       for i in range(2)]
                edT1 = ex_sc.enter_context(nc.sbuf_tensor(f"edT{half}", [128, 4, DM], BF16))
                act = ex_sc.enter_context(nc.sbuf_tensor(f"act{half}", [128, 4, 512], BF16))

                for e in range(NE):
                    slot = e % 2
                    ebn = f"eb{slot}"

                    @block.sync
                    def _(y, half=half, e=e, slot=slot, ebn=ebn):
                        wt(y, "y", [("ss", 16 * 96)])
                        pre(y, "y", writes=(ebn,))
                        stn = f"st{slot}"
                        inc(y.dma_start(egT[slot][:, :, :], eg_st[e], transpose=True), stn, 16)
                        inc(y.dma_start(euT[slot][:, :, :], eu_st[e], transpose=True), stn, 16)
                        post(stn, cnt[stn], writes=(ebn,))
                        pre(y, "y", writes=("edt",))
                        inc(y.dma_start(edT1[:, :, :], ed_st[e], transpose=True), "st2", 16)
                        post("st2", cnt["st2"], writes=("edt",))

                    @block.vector
                    def _(v, e=e):
                        pre(v, "v", reads=("iok",), writes=("sele",))
                        iv = inc(v.tensor_scalar(sel_ee[:, :], iok[:, :], float(e), None,
                                                 ALU.is_equal), "cv")
                        post("cv", iv, reads=("iok",), writes=("sele",))

                    for tt in range(NTH):
                        @block.tensor
                        def _(t, e=e, tt=tt):
                            pre(t, "t", reads=("sele", "wfm"), writes=("pf6",))
                            it = inc(t.matmul(pf[6][:, :], sel_ee[:, :],
                                              w_fm[:, tt * 512:(tt + 1) * 512],
                                              start=True, stop=True), "ct")
                            post("ct", it, reads=("sele", "wfm"), writes=("pf6",))

                        for ic in range(4):
                            pga, pgan = pf[0 + 2 * (ic % 2)], f"pf{0 + 2 * (ic % 2)}"
                            pua, puan = pf[1 + 2 * (ic % 2)], f"pf{1 + 2 * (ic % 2)}"

                            @block.tensor
                            def _(t, e=e, tt=tt, ic=ic, slot=slot, ebn=ebn,
                                  pga=pga, pua=pua, pgan=pgan, puan=puan):
                                pre(t, "t", reads=(ebn, "ln2b"), writes=(pgan, puan))
                                for dc in range(DC):
                                    t.matmul(pga[:, :], egT[slot][:, dc, ic * 128:(ic + 1) * 128],
                                             ln2b[:, dc, tt * 512:(tt + 1) * 512],
                                             start=(dc == 0), stop=(dc == DC - 1))
                                for dc in range(DC):
                                    last = t.matmul(pua[:, :], euT[slot][:, dc, ic * 128:(ic + 1) * 128],
                                                    ln2b[:, dc, tt * 512:(tt + 1) * 512],
                                                    start=(dc == 0), stop=(dc == DC - 1))
                                it = inc(last, "ct")
                                post("ct", it, reads=(ebn, "ln2b"), writes=(pgan, puan))

                            @block.scalar
                            def _(s, e=e, tt=tt, ic=ic, pga=pga, pgan=pgan):
                                pre(s, "s", reads=(pgan,), writes=("sgl",))
                                ic_ = inc(s.activation(sgl[:, :], pga[:, :], AF.Silu), "cs")
                                post("cs", ic_, reads=(pgan,), writes=("sgl",))

                            @block.vector
                            def _(v, e=e, tt=tt, ic=ic, pua=pua, puan=puan):
                                pre(v, "v", reads=("sgl", puan), writes=("sgl",))
                                iv1 = inc(v.tensor_tensor(sgl[:, :], sgl[:, :], pua[:, :], ALU.mult), "cv")
                                post("cv", iv1, reads=(puan,), writes=("sgl",))
                                pre(v, "v", reads=("sgl", "pf6"), writes=("act",))
                                iv2 = inc(v.tensor_tensor(act[:, ic, :], sgl[:, :],
                                                          pf[6][:, :], ALU.mult), "cv")
                                post("cv", iv2, reads=("sgl", "pf6"))
                                res["act"].w["cv"] = iv2

                        for dc in range(DC):
                            pda, pdan = pf[4 + dc % 2], f"pf{4 + dc % 2}"

                            @block.tensor
                            def _(t, e=e, tt=tt, dc=dc, slot=slot, ebn=ebn, pda=pda, pdan=pdan):
                                pre(t, "t", reads=("edt", "act"), writes=(pdan,))
                                for ic in range(4):
                                    last = t.matmul(pda[:, :], edT1[:, ic, dc * 128:(dc + 1) * 128],
                                                    act[:, ic, :],
                                                    start=(ic == 0), stop=(ic == 3))
                                it = inc(last, "ct")
                                post("ct", it, reads=("edt", "act"), writes=(pdan,))

                            @block.vector
                            def _(v, e=e, tt=tt, dc=dc, pda=pda, pdan=pdan):
                                pre(v, "v", reads=(pdan, "routed"), writes=("routed",))
                                iv = inc(v.tensor_tensor(routed[:, dc, tt * 512:(tt + 1) * 512],
                                                         routed[:, dc, tt * 512:(tt + 1) * 512],
                                                         pda[:, :], ALU.add), "cv")
                                post("cv", iv, reads=(pdan,))
                                res["routed"].w["cv"] = iv

                if debug:
                    @block.gpsimd
                    def _(g, half=half):
                        pre(g, "g", reads=("routed",))
                        ig = inc(g.dma_start(dbg["routed"][half], routed[:, :, :]), "sdbg", 16)
                        post("sdbg", ig, reads=("routed",))

                barrier()
                ex_sc.close()
                o_sc = ExitStack()
                out_tm = o_sc.enter_context(nc.sbuf_tensor(f"outtm{half}", [128, 4, DM], F32))

                for tt in range(NTH):
                    gtt = hoff + tt

                    @block.tensor
                    def _(t, tt=tt):
                        pre(t, "t", reads=("routed", "ones128"), writes=("pf0",))
                        for dc in range(DC):
                            last1 = t.matmul(pf[0][0:1, :], ones128[:, :],
                                             routed[:, dc, tt * 512:(tt + 1) * 512],
                                             start=(dc == 0), stop=(dc == DC - 1))
                        it1 = inc(last1, "ct")
                        post("ct", it1, reads=("routed", "ones128"), writes=("pf0",))

                    for dc in range(DC):
                        @block.vector
                        def _(v, tt=tt, dc=dc):
                            pre(v, "v", reads=("routed",), writes=(f"sqf{dc % 2}",))
                            src_ap = routed[:, dc, tt * 512:(tt + 1) * 512]
                            ic_ = inc(v.scalar_tensor_tensor(sqf[dc % 2][:, :], src_ap, 0.0,
                                                             src_ap, ALU.bypass, ALU.mult), "cv")
                            post("cv", ic_, reads=("routed",), writes=(f"sqf{dc % 2}",))

                        @block.tensor
                        def _(t, tt=tt, dc=dc):
                            pre(t, "t", reads=(f"sqf{dc % 2}",), writes=("pf1",) if dc == 0 else ())
                            last2 = t.matmul(pf[1][0:1, :], ones128[:, :], sqf[dc % 2][:, :],
                                             start=(dc == 0), stop=(dc == DC - 1))
                            it2 = inc(last2, "ct")
                            post("ct", it2, reads=(f"sqf{dc % 2}",))
                            res["pf1"].w["ct"] = cnt["ct"]

                    row_finalize(pf[0][0:1, :], pf[1][0:1, :], "pf0", "pf1", DM)

                    @block.tensor
                    def _(t, tt=tt):
                        pre(t, "t", reads=("rows", "onesc"), writes=("pf2", "pf3"))
                        inc(t.matmul(pf[2][:, :], onesc[:, :], murow[0:1, :], start=True, stop=True), "ct")
                        it = inc(t.matmul(pf[3][:, :], onesc[:, :], rrow[0:1, :], start=True, stop=True), "ct")
                        post("ct", it, reads=("rows", "onesc"), writes=("pf2", "pf3"))

                    @block.vector
                    def _(v, tt=tt, gtt=gtt):
                        for dc in range(DC):
                            tbb = f"tbf{dc % 2}"
                            pre(v, "v", reads=("routed", "pf2", "pf3", AR[gtt]), writes=(tbb, "routed"))
                            iv1 = inc(v.tensor_tensor(tbf[dc % 2][:, :],
                                                      routed[:, dc, tt * 512:(tt + 1) * 512],
                                                      pf[2][:, :], ALU.subtract), "cv")
                            wt(v, "v", [("cv", iv1)])
                            iv2 = inc(v.tensor_tensor(tbf[dc % 2][:, :], tbf[dc % 2][:, :], pf[3][:, :],
                                                      ALU.mult), "cv")
                            wt(v, "v", [("cv", iv2)])
                            iv3 = inc(v.tensor_tensor(routed[:, dc, tt * 512:(tt + 1) * 512],
                                                      tbf[dc % 2][:, :], strm_ap(dc, gtt), ALU.add), "cv")
                            post("cv", iv3, reads=("pf2", "pf3", AR[gtt]), writes=(tbb,))
                            res["routed"].w["cv"] = iv3

                    for j in range(4):
                        @block.tensor
                        def _(t, tt=tt, j=j):
                            pre(t, "t", reads=("routed", "identf"), writes=("pf6", "pf7"))
                            for dc in range(DC):
                                bank = pf[6] if dc < 4 else pf[7]
                                last = t.transpose(bank[:, (dc % 4) * 128:(dc % 4 + 1) * 128],
                                                   routed[:, dc, tt * 512 + j * 128:tt * 512 + (j + 1) * 128],
                                                   identf[:, :])
                            it = inc(last, "ct")
                            post("ct", it, reads=("routed", "identf"), writes=("pf6", "pf7"))

                        @block.scalar
                        def _(s, tt=tt, j=j):
                            pre(s, "s", reads=("pf6", "pf7"), writes=("outtm",))
                            inc(s.copy(out_tm[:, j, 0:512], pf[6][:, :]), "cs")
                            ic_ = inc(s.copy(out_tm[:, j, 512:1024], pf[7][:, :]), "cs")
                            post("cs", ic_, reads=("pf6", "pf7"))
                            res["outtm"].w["cs"] = ic_

                    @block.sync
                    def _(y, tt=tt, gtt=gtt):
                        pre(y, "y", reads=("outtm",))
                        dst = bass.AP(out_d.tensor, gtt * 512 * DM, [[DM, 128], [128 * DM, 4], [1, DM]])
                        iy = inc(y.dma_start(dst, out_tm[:, :, :]), "so", 16)
                        post("so", iy, reads=("outtm",))

                barrier()
                o_sc.close()

            @block.sync
            def _(y):
                wt(y, "y", [("so", cnt["so"])])
                if debug:
                    pass

            if debug:
                @block.gpsimd
                def _(g):
                    g.wait_ge(sems["sdbg"], cnt["sdbg"])

    es.close()
    return nc


def kernel(**inputs):
    x = np.asarray(inputs["L0_attn_x"], dtype=np.float32)
    base = {
        "rms": np.ascontiguousarray(np.asarray(inputs["L0_attn_rms_scale"], np.float32).reshape(1, DM)),
        "spl": np.ascontiguousarray(np.asarray(inputs["L0_attn_split_scale"], np.float32).reshape(1, DM)),
        "wk": np.ascontiguousarray(np.asarray(inputs["L0_attn_W_K"], np.float32)),
        "wugg": np.ascontiguousarray(np.asarray(inputs["L0_attn_W_ugg"], np.float32)),
        "wout": np.ascontiguousarray(np.asarray(inputs["L0_attn_W_out"], np.float32)),
        "gw": np.ascontiguousarray(np.asarray(inputs["L0_ffn_gate_w"], np.float32)),
        "gb": np.ascontiguousarray(np.asarray(inputs["L0_ffn_gate_bias"], np.float32).reshape(1, NE)),
        "eg": np.ascontiguousarray(np.asarray(inputs["L0_ffn_expert_gate"], np.float32)),
        "eu": np.ascontiguousarray(np.asarray(inputs["L0_ffn_expert_up"], np.float32)),
        "ed": np.ascontiguousarray(np.asarray(inputs["L0_ffn_expert_down"], np.float32)),
        "shg": np.ascontiguousarray(np.asarray(inputs["L0_ffn_shared_gate"], np.float32)),
        "shu": np.ascontiguousarray(np.asarray(inputs["L0_ffn_shared_up"], np.float32)),
        "shd": np.ascontiguousarray(np.asarray(inputs["L0_ffn_shared_down"], np.float32)),
    }
    in_maps = []
    for c in range(8):
        m = dict(base)
        m["x"] = np.ascontiguousarray(x[c])
        in_maps.append(m)
    nc = build(debug=False)
    res_ = run_bass_kernel_spmd(nc, in_maps, core_ids=list(range(8)))
    return np.stack([r["out"] for r in res_.results], axis=0)


# version 04
# speedup vs baseline: 1.1602x; 1.1602x over previous
import numpy as np
from contextlib import ExitStack

import concourse.bass as bass
import concourse.mybir as mybir
from concourse.bass_utils import run_bass_kernel_spmd

F32 = mybir.dt.float32
BF16 = mybir.dt.bfloat16
AF = mybir.ActivationFunctionType
ALU = mybir.AluOpType
AXX = mybir.AxisListType.X

S = 2048
DM = 1024
HH = 2048
NE = 32
NI = 512
DC = 8
HCN = 16
NT = 4
SH = S // 2
NTH = 2
LN_EPS = 1e-5
NORM_EPS = 1e-8


class _Res:
    __slots__ = ("w", "r")

    def __init__(self):
        self.w = {}
        self.r = {}


def build(debug=False):
    nc = bass.Bass("TRN2", target_bir_lowering=False)

    x_d = nc.dram_tensor("x", [S, DM], F32, kind="ExternalInput").ap()
    rms_d = nc.dram_tensor("rms", [1, DM], F32, kind="ExternalInput").ap()
    spl_d = nc.dram_tensor("spl", [1, DM], F32, kind="ExternalInput").ap()
    wk_d = nc.dram_tensor("wk", [DM, HH], F32, kind="ExternalInput").ap()
    wugg_d = nc.dram_tensor("wugg", [DM, 3 * HH], F32, kind="ExternalInput").ap()
    wout_d = nc.dram_tensor("wout", [HH, DM], F32, kind="ExternalInput").ap()
    gw_d = nc.dram_tensor("gw", [NE, DM], F32, kind="ExternalInput").ap()
    gb_d = nc.dram_tensor("gb", [1, NE], F32, kind="ExternalInput").ap()
    eg_d = nc.dram_tensor("eg", [NE, NI, DM], F32, kind="ExternalInput").ap()
    eu_d = nc.dram_tensor("eu", [NE, NI, DM], F32, kind="ExternalInput").ap()
    ed_d = nc.dram_tensor("ed", [NE, DM, NI], F32, kind="ExternalInput").ap()
    shg_d = nc.dram_tensor("shg", [DM, DM], F32, kind="ExternalInput").ap()
    shu_d = nc.dram_tensor("shu", [DM, DM], F32, kind="ExternalInput").ap()
    shd_d = nc.dram_tensor("shd", [DM, DM], F32, kind="ExternalInput").ap()
    out_d = nc.dram_tensor("out", [S, DM], F32, kind="ExternalOutput").ap()

    eg_st = nc.dram_tensor("eg_st", [NE, NI, DM], BF16, kind="Internal").ap()
    eu_st = nc.dram_tensor("eu_st", [NE, NI, DM], BF16, kind="Internal").ap()
    ed_st = nc.dram_tensor("ed_st", [NE, DM, NI], BF16, kind="Internal").ap()
    shg_st = nc.dram_tensor("shg_st", [DM, DM], BF16, kind="Internal").ap()
    shu_st = nc.dram_tensor("shu_st", [DM, DM], BF16, kind="Internal").ap()
    shd_st = nc.dram_tensor("shd_st", [DM, DM], BF16, kind="Internal").ap()

    dbg = {}
    if debug:
        dbg["xn"] = nc.dram_tensor("d_xn", [128, DC, S], BF16, kind="ExternalOutput").ap()
        dbg["gated"] = nc.dram_tensor("d_gated", [128, NT, HCN * 512], BF16, kind="ExternalOutput").ap()
        dbg["strm"] = nc.dram_tensor("d_strm", [128, NT, DC * 512], F32, kind="ExternalOutput").ap()
        dbg["ln2"] = nc.dram_tensor("d_ln2", [2, 128, DC, SH], BF16, kind="ExternalOutput").ap()
        dbg["lgf"] = nc.dram_tensor("d_lgf", [2, NE, SH], F32, kind="ExternalOutput").ap()
        dbg["wfm"] = nc.dram_tensor("d_wfm", [2, NE, SH], F32, kind="ExternalOutput").ap()
        dbg["routed"] = nc.dram_tensor("d_routed", [2, 128, DC, SH], F32, kind="ExternalOutput").ap()

    es = ExitStack()

    sems = {}
    cnt = {}
    for name in ("ct", "cv", "cs", "cg", "syc", "sx0", "sx1", "sgw0", "sgw1", "sgo",
                 "ss", "ss2", "st0", "st1", "st2", "st3", "so", "sdbg"):
        sems[name] = es.enter_context(nc.semaphore(name))
        cnt[name] = 0
    wm = {e: {} for e in ("t", "v", "s", "g", "y")}

    import collections
    res = collections.defaultdict(_Res)

    def wt(eng, en, deps):
        coll = {}
        for name, val in deps:
            if not val:
                continue
            if coll.get(name, 0) < val:
                coll[name] = val
        for name, val in coll.items():
            if wm[en].get(name, 0) >= val:
                continue
            wm[en][name] = val
            eng.wait_ge(sems[name], val)

    def pre(eng, en, reads=(), writes=()):
        deps = []
        for b in reads:
            deps.extend(res[b].w.items())
        for b in writes:
            deps.extend(res[b].w.items())
            deps.extend(res[b].r.items())
        wt(eng, en, deps)

    def post(sem, val, reads=(), writes=()):
        for b in reads:
            r = res[b].r
            if r.get(sem, 0) < val:
                r[sem] = val
        for b in writes:
            w = res[b].w
            if w.get(sem, 0) < val:
                w[sem] = val

    def inc(instr, name, k=1):
        cnt[name] += k
        instr.then_inc(sems[name], k)
        return cnt[name]

    def barrier():
        deps = [("ct", cnt["ct"]), ("cv", cnt["cv"]), ("cs", cnt["cs"]), ("cg", cnt["cg"]),
                ("sdbg", cnt["sdbg"]), ("so", cnt["so"])]

        @block.tensor
        def _(t):
            wt(t, "t", deps)

        @block.vector
        def _(v):
            wt(v, "v", deps)

        @block.scalar
        def _(s):
            wt(s, "s", deps)

        @block.gpsimd
        def _(g):
            wt(g, "g", deps)

        @block.sync
        def _(y):
            wt(y, "y", deps)

    identf = es.enter_context(nc.sbuf_tensor("identf", [128, 128], F32))
    identb = es.enter_context(nc.sbuf_tensor("identb", [128, 128], BF16))
    ones128 = es.enter_context(nc.sbuf_tensor("ones128", [128, 1], F32))
    onesc = es.enter_context(nc.sbuf_tensor("onesc", [1, 128], F32))
    iok = es.enter_context(nc.sbuf_tensor("iok", [NE, 128], F32))
    sel_ee = es.enter_context(nc.sbuf_tensor("sel_ee", [NE, 128], F32))
    gwT = es.enter_context(nc.sbuf_tensor("gwT", [128, DC, NE], F32))
    gbb = es.enter_context(nc.sbuf_tensor("gbb", [1, NE], F32))
    bias_rep = es.enter_context(nc.sbuf_tensor("bias_rep", [128, NE], F32))
    murow = es.enter_context(nc.sbuf_tensor("murow", [1, 512], F32))
    rrow = es.enter_context(nc.sbuf_tensor("rrow", [1, 512], F32))
    rowsc = es.enter_context(nc.sbuf_tensor("rowsc", [1, 512 * 3], F32))
    arena = es.enter_context(nc.sbuf_tensor("arena", [128, NT, DC * 512], F32))
    arena_b = arena[:, :, :].bitcast(BF16)

    def gated_ap(hc, tt=None):
        if tt is None:
            return arena_b[:, :, hc * 512:(hc + 1) * 512]
        return arena_b[:, tt, hc * 512:(hc + 1) * 512]

    def strm_ap(dc, gtt):
        return arena[:, gtt, dc * 512:(dc + 1) * 512]

    AR = ["ar0", "ar1", "ar2", "ar3"]
    C = [rowsc[0:1, 512 * i:512 * (i + 1)] for i in range(3)]

    def row_finalize(s1ap, s2ap, bank1, bank2, n):
        @block.vector
        def _(v):
            pre(v, "v", reads=(bank1, bank2), writes=("rows",))
            i0 = inc(v.tensor_scalar(murow[:, :], s1ap, 1.0 / n, None, ALU.mult), "cv")
            wt(v, "v", [("cv", i0)])
            i1 = inc(v.tensor_scalar(C[0], s2ap, 1.0 / n, None, ALU.mult), "cv")
            wt(v, "v", [("cv", i1)])
            i2 = inc(v.tensor_tensor(C[1], murow[:, :], murow[:, :], ALU.mult), "cv")
            wt(v, "v", [("cv", i2)])
            i3 = inc(v.tensor_tensor(C[1], C[0], C[1], ALU.subtract), "cv")
            wt(v, "v", [("cv", i3)])
            i4 = inc(v.tensor_scalar(C[0], C[1], LN_EPS, None, ALU.add), "cv")
            post("cv", i4, reads=(bank1, bank2), writes=("rows",))
        rv1 = cnt["cv"]

        @block.scalar
        def _(s):
            wt(s, "s", [("cv", rv1)])
            pre(s, "s", writes=("rows2",))
            i5 = inc(s.activation(C[2], C[0], AF.Sqrt), "cs")
            post("cs", i5, writes=("rows2",))
        rs1 = cnt["cs"]

        @block.vector
        def _(v):
            wt(v, "v", [("cs", rs1)])
            i6 = inc(v.reciprocal(C[1], C[2]), "cv")
            wt(v, "v", [("cv", i6)])
            i7 = inc(v.tensor_tensor(C[2], C[0], C[1], ALU.mult), "cv")
            wt(v, "v", [("cv", i7)])
            i8 = inc(v.tensor_tensor(C[2], C[2], C[1], ALU.mult), "cv")
            wt(v, "v", [("cv", i8)])
            i9 = inc(v.tensor_scalar(C[2], C[2], -0.5, 1.5, ALU.mult, ALU.add), "cv")
            wt(v, "v", [("cv", i9)])
            i10 = inc(v.tensor_tensor(rrow[:, :], C[1], C[2], ALU.mult), "cv")
            post("cv", i10, writes=("rows",))
        return cnt["cv"]

    swout = ExitStack()
    wout_sb = swout.enter_context(nc.sbuf_tensor("wout_sb", [128, HCN, DM], BF16))
    sxn = ExitStack()
    xn_fm = sxn.enter_context(nc.sbuf_tensor("xn_fm", [128, DC, S], BF16))
    sio = ExitStack()
    io1 = sio.enter_context(nc.sbuf_tensor("io1", [128, 128], F32))
    io2 = sio.enter_context(nc.sbuf_tensor("io2", [128, 128], F32))
    gwb = sio.enter_context(nc.sbuf_tensor("gwb", [NE, DM], F32))

    with nc.Block() as block:
        @block.gpsimd
        def _(g):
            i1 = inc(g.iota(io1[:, :], [[1, 128]], channel_multiplier=0,
                            allow_small_or_imprecise_dtypes=True), "cg")
            post("cg", i1, writes=("io1",))
            i2 = inc(g.iota(io2[:, :], [[0, 128]], channel_multiplier=1,
                            allow_small_or_imprecise_dtypes=True), "cg")
            post("cg", i2, writes=("io2",))
            i3 = inc(g.iota(iok[:, :], [[0, 128]], channel_multiplier=1,
                            allow_small_or_imprecise_dtypes=True), "cg")
            post("cg", i3, writes=("iok",))

        @block.sync
        def _(y):
            inc(y.dma_start(gwb[:, :], gw_d), "syc", 16)
            inc(y.dma_start(gbb[:, :], gb_d), "syc", 16)

        @block.vector
        def _(v):
            pre(v, "v", reads=("io1", "io2"), writes=("identf",))
            i1 = inc(v.tensor_tensor(identf[:, :], io1[:, :], io2[:, :], ALU.is_equal), "cv")
            post("cv", i1, reads=("io1", "io2"), writes=("identf",))
            wt(v, "v", [("cv", i1)])
            i2 = inc(v.tensor_copy(identb[:, :], identf[:, :]), "cv")
            post("cv", i2, reads=("identf",), writes=("identb",))
            i3 = inc(v.memset(ones128[:, :], 1.0), "cv")
            post("cv", i3, writes=("ones128",))
            i4 = inc(v.memset(onesc[:, :], 1.0), "cv")
            post("cv", i4, writes=("onesc",))
        ST_SH = 16 * 3
        ST_EXP = [16 * (3 + 3 * (e + 1)) for e in range(NE)]

        with ExitStack() as s0:
            pk0 = s0.enter_context(nc.psum_tensor("pk0", [128, 512], F32))
            pk1 = s0.enter_context(nc.psum_tensor("pk1", [128, 512], F32))
            pbb = s0.enter_context(nc.psum_tensor("pbb", [128, 2048], BF16))
            rmsb = s0.enter_context(nc.sbuf_tensor("rmsb", [1, DM], F32))
            splb = s0.enter_context(nc.sbuf_tensor("splb", [1, DM], F32))
            crow = s0.enter_context(nc.sbuf_tensor("crow", [1, DM], F32))
            c_rep = s0.enter_context(nc.sbuf_tensor("c_rep", [128, DM], F32))
            x4 = [s0.enter_context(nc.sbuf_tensor(f"x4_{i}", [128, 4, DM], F32)) for i in range(2)]
            sqd = s0.enter_context(nc.sbuf_tensor("sqd_a0", [128, DM], F32))
            xn_tmp = s0.enter_context(nc.sbuf_tensor("xn_tmp", [128, DM], BF16))
            xn1 = s0.enter_context(nc.sbuf_tensor("xn1_a0", [128, DM], F32))
            sums = s0.enter_context(nc.sbuf_tensor("sums", [128, 16], F32))
            sumsq = s0.enter_context(nc.sbuf_tensor("sumsq", [128, 16], F32))
            mu_a = s0.enter_context(nc.sbuf_tensor("mu_a", [128, 16], F32))
            ap_a = s0.enter_context(nc.sbuf_tensor("ap_a", [128, 16], F32))
            g_sc = s0.enter_context(nc.sbuf_tensor("g_sc", [128, 8 * 16], F32))
            GC = [g_sc[:, 16 * i:16 * (i + 1)] for i in range(8)]

            @block.sync
            def _(y):
                inc(y.dma_start(rmsb[:, :], rms_d), "syc", 16)
                post("syc", inc(y.dma_start(splb[:, :], spl_d), "syc", 16), writes=("splb",))

            @block.vector
            def _(v):
                pre(v, "v", reads=("rmsb", "splb"), writes=("crow",))
                i1 = inc(v.tensor_tensor(crow[:, :], rmsb[:, :], splb[:, :], ALU.mult), "cv")
                post("cv", i1, reads=("rmsb", "splb"), writes=("crow",))

            @block.tensor
            def _(t):
                pre(t, "t", reads=("onesc", "crow"), writes=("pk0", "pk1"))
                i0 = inc(t.matmul(pk0[:, :], onesc[:, :], crow[0:1, 0:512], start=True, stop=True), "ct")
                i1 = inc(t.matmul(pk1[:, :], onesc[:, :], crow[0:1, 512:1024], start=True, stop=True), "ct")
                post("ct", i1, reads=("onesc", "crow"), writes=("pk0", "pk1"))

            @block.scalar
            def _(s):
                pre(s, "s", reads=("pk0", "pk1"), writes=("c_rep",))
                inc(s.copy(c_rep[:, 0:512], pk0[:, :]), "cs")
                i1 = inc(s.copy(c_rep[:, 512:1024], pk1[:, :]), "cs")
                post("cs", i1, reads=("pk0", "pk1"), writes=("c_rep",))

            @block.tensor
            def _(t):
                pre(t, "t", reads=("onesc", "gbb", "gwb", "identf"), writes=("pk0", "pk1"))
                inc(t.matmul(pk0[0:128, 0:NE], onesc[:, :], gbb[:, :], start=True, stop=True), "ct")
                for c in range(DC):
                    last = t.transpose(pk1[:, c * NE:(c + 1) * NE], gwb[:, c * 128:(c + 1) * 128],
                                       identf[0:NE, 0:NE])
                i1 = inc(last, "ct")
                post("ct", i1, reads=("onesc", "gbb", "gwb", "identf"), writes=("pk0", "pk1"))

            @block.scalar
            def _(s):
                pre(s, "s", reads=("pk0", "pk1"), writes=("bias_rep", "gwT"))
                inc(s.copy(bias_rep[:, :], pk0[0:128, 0:NE]), "cs")
                i1 = inc(s.copy(gwT[:, :, :],
                                pk1[:, 0:DC * NE].rearrange("p (c e) -> p c e", e=NE)), "cs")
                post("cs", i1, reads=("pk0", "pk1"), writes=("bias_rep", "gwT"))

            for grp in range(4):
                slot = grp % 2
                xb = f"x4_{slot}"

                @block.sync
                def _(y, grp=grp, slot=slot, xb=xb):
                    pre(y, "y", writes=(xb,))
                    src = bass.AP(x_d.tensor, grp * 512 * DM, [[DM, 128], [128 * DM, 4], [1, DM]])
                    sxn_ = f"sx{slot}"
                    post(sxn_, inc(y.dma_start(x4[slot][:, :, :], src), sxn_, 16), writes=(xb,))

                for j in range(4):
                    i = grp * 4 + j

                    @block.vector
                    def _(v, i=i, j=j, slot=slot, xb=xb):
                        pre(v, "v", reads=(xb,), writes=("a0st",))
                        iv = inc(v.reduce_sum(sums[:, i:i + 1], x4[slot][:, j, :], axis=AXX), "cv")
                        post("cv", iv, reads=(xb,), writes=("a0st",))

                    @block.vector
                    def _(v, i=i, j=j, slot=slot, xb=xb):
                        pre(v, "v", reads=(xb,), writes=("sqd", "a0st"))
                        ix = inc(v.scalar_tensor_tensor(sqd[:, :], x4[slot][:, j, :], 0.0,
                                                        x4[slot][:, j, :], ALU.bypass, ALU.mult,
                                                        accum_out=sumsq[:, i:i + 1]), "cv")
                        post("cv", ix, reads=(xb,), writes=("sqd", "a0st"))

                sl = slice(grp * 4, grp * 4 + 4)

                @block.vector
                def _(v, sl=sl):
                    pre(v, "v", reads=("a0st",), writes=("a0st",))
                    i0 = inc(v.tensor_scalar(mu_a[:, sl], sums[:, sl], 1.0 / DM, None, ALU.mult), "cv")
                    wt(v, "v", [("cv", i0)])
                    i1 = inc(v.tensor_scalar(GC[0][:, sl], sumsq[:, sl], 1.0 / DM, None, ALU.mult), "cv")
                    wt(v, "v", [("cv", i1)])
                    i2 = inc(v.tensor_tensor(GC[1][:, sl], mu_a[:, sl], mu_a[:, sl], ALU.mult), "cv")
                    wt(v, "v", [("cv", i2)])
                    i3 = inc(v.tensor_tensor(GC[1][:, sl], GC[0][:, sl], GC[1][:, sl], ALU.subtract), "cv")
                    wt(v, "v", [("cv", i3)])
                    i4 = inc(v.tensor_scalar(GC[2][:, sl], GC[1][:, sl], LN_EPS, None, ALU.add), "cv")
                    post("cv", i4, writes=("a0st",))
                a0f1 = cnt["cv"]

                @block.scalar
                def _(s, sl=sl):
                    wt(s, "s", [("cv", a0f1)])
                    i5 = inc(s.activation(GC[3][:, sl], GC[2][:, sl], AF.Sqrt), "cs")
                    post("cs", i5, writes=("a0st2",))
                a0f2 = cnt["cs"]

                @block.vector
                def _(v, sl=sl):
                    wt(v, "v", [("cs", a0f2)])
                    i6 = inc(v.reciprocal(GC[4][:, sl], GC[3][:, sl]), "cv")
                    wt(v, "v", [("cv", i6)])
                    i7 = inc(v.tensor_tensor(GC[5][:, sl], GC[2][:, sl], GC[4][:, sl], ALU.mult), "cv")
                    wt(v, "v", [("cv", i7)])
                    i8 = inc(v.tensor_tensor(GC[5][:, sl], GC[5][:, sl], GC[4][:, sl], ALU.mult), "cv")
                    wt(v, "v", [("cv", i8)])
                    i9 = inc(v.tensor_scalar(GC[5][:, sl], GC[5][:, sl], -0.5, 1.5, ALU.mult, ALU.add), "cv")
                    wt(v, "v", [("cv", i9)])
                    i10 = inc(v.tensor_tensor(GC[4][:, sl], GC[4][:, sl], GC[5][:, sl], ALU.mult), "cv")
                    wt(v, "v", [("cv", i10)])
                    i11 = inc(v.tensor_tensor(GC[6][:, sl], GC[1][:, sl], GC[4][:, sl], ALU.mult), "cv")
                    wt(v, "v", [("cv", i11)])
                    i12 = inc(v.tensor_tensor(GC[6][:, sl], GC[6][:, sl], GC[4][:, sl], ALU.mult), "cv")
                    post("cv", i12, writes=("a0st",))
                a0f3 = cnt["cv"]

                @block.scalar
                def _(s, sl=sl):
                    wt(s, "s", [("cv", a0f3)])
                    i13 = inc(s.activation(GC[7][:, sl], GC[6][:, sl], AF.Sqrt), "cs")
                    post("cs", i13, writes=("a0st2",))
                a0f4 = cnt["cs"]

                @block.vector
                def _(v, sl=sl):
                    wt(v, "v", [("cs", a0f4)])
                    i14 = inc(v.tensor_scalar(GC[7][:, sl], GC[7][:, sl], 1e-20, None, ALU.max), "cv")
                    wt(v, "v", [("cv", i14)])
                    i15 = inc(v.reciprocal(GC[3][:, sl], GC[7][:, sl]), "cv")
                    wt(v, "v", [("cv", i15)])
                    i16 = inc(v.tensor_tensor(GC[3][:, sl], GC[6][:, sl], GC[3][:, sl], ALU.mult), "cv")
                    wt(v, "v", [("cv", i16)])
                    i17 = inc(v.tensor_tensor(GC[3][:, sl], GC[7][:, sl], GC[3][:, sl], ALU.add), "cv")
                    wt(v, "v", [("cv", i17)])
                    i18 = inc(v.tensor_scalar(GC[3][:, sl], GC[3][:, sl], 0.5, NORM_EPS, ALU.mult, ALU.add), "cv")
                    wt(v, "v", [("cv", i18)])
                    i19 = inc(v.reciprocal(GC[3][:, sl], GC[3][:, sl]), "cv")
                    wt(v, "v", [("cv", i19)])
                    i20 = inc(v.tensor_tensor(ap_a[:, sl], GC[4][:, sl], GC[3][:, sl], ALU.mult), "cv")
                    post("cv", i20, writes=("a0st",))

                for j in range(4):
                    i = grp * 4 + j

                    @block.vector
                    def _(v, i=i, j=j, slot=slot, xb=xb):
                        pre(v, "v", reads=(xb, "a0st", "c_rep"), writes=("xn1", "xn_tmp"))
                        iv1 = inc(v.tensor_scalar(xn1[:, :], x4[slot][:, j, :], mu_a[:, i:i + 1],
                                                  ap_a[:, i:i + 1], ALU.subtract, ALU.mult), "cv")
                        wt(v, "v", [("cv", iv1)])
                        iv2 = inc(v.tensor_tensor(xn_tmp[:, :], xn1[:, :], c_rep[:, :], ALU.mult), "cv")
                        post("cv", iv2, reads=(xb, "a0st", "c_rep"), writes=("xn1", "xn_tmp"))

                    pb = f"pbb{i % 2}"
                    half0 = (i % 2) * 1024

                    @block.tensor
                    def _(t, i=i, half0=half0, pb=pb):
                        pre(t, "t", reads=("xn_tmp", "identb"), writes=(pb,))
                        for c in range(DC):
                            last = t.transpose(pbb[:, half0 + c * 128: half0 + (c + 1) * 128],
                                               xn_tmp[:, c * 128:(c + 1) * 128], identb[:, :])
                        it = inc(last, "ct")
                        post("ct", it, reads=("xn_tmp", "identb"), writes=(pb,))

                    @block.scalar
                    def _(s, i=i, half0=half0, pb=pb):
                        pre(s, "s", reads=(pb,), writes=("xn",))
                        ic_ = inc(s.copy(xn_fm[:, :, i * 128:(i + 1) * 128],
                                         pbb[:, half0:half0 + 1024].rearrange("p (c k) -> p c k", k=128)), "cs")
                        post("cs", ic_, reads=(pb,), writes=("xn",))
                        res["xn"].w["cs"] = ic_

            if debug:
                @block.gpsimd
                def _(g):
                    pre(g, "g", reads=("xn",))
                    post("sdbg", inc(g.dma_start(dbg["xn"], xn_fm[:, :, :]), "sdbg", 16), reads=("xn",))
            barrier()

        sio.close()

        with ExitStack() as s1:
            pj = [s1.enter_context(nc.psum_tensor(f"pj{i}", [128, 512], F32)) for i in range(8)]
            wkb = [s1.enter_context(nc.sbuf_tensor(f"wkb{i}", [128, 8, 128], BF16)) for i in range(2)]
            wub = [s1.enter_context(nc.sbuf_tensor(f"wub{i}", [128, 8, 3, 128], BF16)) for i in range(2)]
            K_sb = s1.enter_context(nc.sbuf_tensor("K_sb", [128, S], F32))
            go_sb = s1.enter_context(nc.sbuf_tensor("go_sb", [128, S], F32))
            b_sb = s1.enter_context(nc.sbuf_tensor("b_sb", [128, S], F32))
            sgi = s1.enter_context(nc.sbuf_tensor("sgi", [128, 512], F32))
            kst0 = s1.enter_context(nc.sbuf_tensor("kst0", [128, 512], F32))
            t1s = [s1.enter_context(nc.sbuf_tensor(f"t1s{i}", [128, 512], F32)) for i in range(2)]

            for hc in range(HCN):
                slot = hc % 2
                wsl = f"a1w{slot}"

                @block.gpsimd
                def _(g, hc=hc, slot=slot, wsl=wsl):
                    pre(g, "g", writes=(wsl,))
                    sgn = f"sgw{slot}"
                    srcap = bass.AP(wk_d.tensor, hc * 128, [[HH, 128], [128 * HH, 8], [1, 128]])
                    inc(g.dma_start(wkb[slot][:, :, :], srcap), sgn, 16)
                    for b in range(3):
                        srcap = bass.AP(wugg_d.tensor, b * HH + hc * 128,
                                        [[3 * HH, 128], [128 * 3 * HH, 8], [1, 128]])
                        inc(g.dma_start(wub[slot][:, :, b, :], srcap), sgn, 16)
                    post(sgn, cnt[sgn], writes=(wsl,))
                    if hc == 0:
                        inc(g.dma_start(shg_st, shg_d), "ss2", 16)
                        inc(g.dma_start(shu_st, shu_d), "ss2", 16)
                        inc(g.dma_start(shd_st, shd_d), "ss2", 16)
                    for e in (hc,):
                        inc(g.dma_start(eg_st[e], eg_d[e]), "ss", 16)
                        inc(g.dma_start(eu_st[e], eu_d[e]), "ss", 16)
                        inc(g.dma_start(ed_st[e], ed_d[e]), "ss", 16)

                if hc == 1:
                    @block.gpsimd
                    def _(g):
                        pre(g, "g", writes=("woutsb",))
                        src = bass.AP(wout_d.tensor, 0, [[DM, 128], [128 * DM, 16], [1, DM]])
                        post("sgo", inc(g.dma_start(wout_sb[:, :, :], src), "sgo", 16),
                             writes=("woutsb",))

                @block.tensor
                def _(t, hc=hc, slot=slot, wsl=wsl):
                    for tt in range(NT):
                        pre(t, "t", reads=("xn", wsl), writes=(f"pj{tt}",))
                        for kc in range(DC):
                            last = t.matmul(pj[tt][:, :], wkb[slot][:, kc, :],
                                            xn_fm[:, kc, tt * 512:(tt + 1) * 512],
                                            start=(kc == 0), stop=(kc == DC - 1))
                        it = inc(last, "ct")
                        post("ct", it, reads=("xn", wsl), writes=(f"pj{tt}",))
                    for tt in range(NT):
                        pre(t, "t", reads=("xn", wsl), writes=(f"pj{4 + tt}",))
                        for kc in range(DC):
                            last = t.matmul(pj[4 + tt][:, :], wub[slot][:, kc, 0, :],
                                            xn_fm[:, kc, tt * 512:(tt + 1) * 512],
                                            start=(kc == 0), stop=(kc == DC - 1))
                        it = inc(last, "ct")
                        post("ct", it, reads=("xn", wsl), writes=(f"pj{4 + tt}",))

                @block.scalar
                def _(s, hc=hc):
                    for tt in range(NT):
                        pre(s, "s", reads=(f"pj{tt}",), writes=("Ksb",))
                        ic_ = inc(s.activation(K_sb[:, tt * 512:(tt + 1) * 512], pj[tt][:, :],
                                               AF.Sigmoid), "cs")
                        post("cs", ic_, reads=(f"pj{tt}",), writes=("Ksb",))
                        res["Ksb"].w["cs"] = ic_

                @block.tensor
                def _(t, hc=hc, slot=slot, wsl=wsl):
                    for tt in range(NT):
                        pre(t, "t", reads=("xn", wsl), writes=(f"pj{tt}",))
                        for kc in range(DC):
                            last = t.matmul(pj[tt][:, :], wub[slot][:, kc, 1, :],
                                            xn_fm[:, kc, tt * 512:(tt + 1) * 512],
                                            start=(kc == 0), stop=(kc == DC - 1))
                        it = inc(last, "ct")
                        post("ct", it, reads=("xn", wsl), writes=(f"pj{tt}",))

                for tt in range(NT):
                    @block.scalar
                    def _(s, hc=hc, tt=tt):
                        pre(s, "s", reads=(f"pj{tt}",), writes=("sgi",))
                        ic_ = inc(s.activation(sgi[:, :], pj[tt][:, :], AF.Sigmoid), "cs")
                        post("cs", ic_, reads=(f"pj{tt}",), writes=("sgi",))

                    tb = f"t1s{tt % 2}"

                    @block.vector
                    def _(v, hc=hc, tt=tt, tb=tb):
                        pre(v, "v", reads=("sgi", f"pj{4 + tt}"), writes=(tb,))
                        iv1 = inc(v.tensor_tensor(t1s[tt % 2][:, :], pj[4 + tt][:, :], sgi[:, :],
                                                  ALU.mult), "cv")
                        post("cv", iv1, reads=("sgi", f"pj{4 + tt}"), writes=(tb,))
                        pre(v, "v", reads=(tb, "Ksb"), writes=("bsb",))
                        iv2 = inc(v.tensor_tensor(b_sb[:, tt * 512:(tt + 1) * 512], t1s[tt % 2][:, :],
                                                  K_sb[:, tt * 512:(tt + 1) * 512], ALU.mult), "cv")
                        wt(v, "v", [("cv", iv2)])
                        iv3 = inc(v.tensor_tensor(b_sb[:, tt * 512:(tt + 1) * 512], t1s[tt % 2][:, :],
                                                  b_sb[:, tt * 512:(tt + 1) * 512], ALU.subtract), "cv")
                        post("cv", iv3, reads=(tb, "Ksb"), writes=("bsb",))
                        res["bsb"].w["cv"] = iv3

                @block.tensor
                def _(t, hc=hc, slot=slot, wsl=wsl):
                    for tt in range(NT):
                        pre(t, "t", reads=("xn", wsl), writes=(f"pj{4 + tt}",))
                        for kc in range(DC):
                            last = t.matmul(pj[4 + tt][:, :], wub[slot][:, kc, 2, :],
                                            xn_fm[:, kc, tt * 512:(tt + 1) * 512],
                                            start=(kc == 0), stop=(kc == DC - 1))
                        it = inc(last, "ct")
                        post("ct", it, reads=("xn", wsl), writes=(f"pj{4 + tt}",))

                @block.scalar
                def _(s, hc=hc):
                    for tt in range(NT):
                        pre(s, "s", reads=(f"pj{4 + tt}",), writes=("gosb",))
                        ic_ = inc(s.activation(go_sb[:, tt * 512:(tt + 1) * 512], pj[4 + tt][:, :],
                                               AF.Sigmoid), "cs")
                        post("cs", ic_, reads=(f"pj{4 + tt}",), writes=("gosb",))
                        res["gosb"].w["cs"] = ic_


                @block.vector
                def _(v, hc=hc):
                    pre(v, "v", reads=("Ksb",), writes=("kst0",))
                    iv0 = inc(v.tensor_copy(kst0[:, 1:512], K_sb[:, 0:511]), "cv")
                    wt(v, "v", [("cv", iv0)])
                    iv = inc(v.memset(kst0[:, 0:1], 0.0), "cv")
                    post("cv", iv, reads=("Ksb",), writes=("kst0",))
                    for tt in range(NT):
                        pre(v, "v", reads=("kst0", "Ksb", "bsb"), writes=("bsb",))
                        if tt == 0:
                            a_ap = kst0[:, :]
                            init = 0.0
                        else:
                            a_ap = K_sb[:, tt * 512 - 1:(tt + 1) * 512 - 1]
                            init = b_sb[:, tt * 512 - 1:tt * 512]
                        iv2 = inc(v.tensor_tensor_scan(b_sb[:, tt * 512:(tt + 1) * 512], a_ap,
                                                       b_sb[:, tt * 512:(tt + 1) * 512], init,
                                                       ALU.mult, ALU.add), "cv")
                        post("cv", iv2, reads=("kst0", "Ksb"), writes=("bsb",))
                    pre(v, "v", reads=("bsb", "gosb"), writes=AR)
                    iv3 = inc(v.tensor_tensor(gated_ap(hc),
                                              b_sb[:, :].rearrange("p (tt k) -> p tt k", k=512),
                                              go_sb[:, :].rearrange("p (tt k) -> p tt k", k=512),
                                              ALU.mult), "cv")
                    post("cv", iv3, reads=("bsb", "gosb"))
                    for a in AR:
                        res[a].w["cv"] = iv3

            @block.gpsimd
            def _(g):
                for e in range(HCN, NE):
                    inc(g.dma_start(eg_st[e], eg_d[e]), "ss", 16)
                    inc(g.dma_start(eu_st[e], eu_d[e]), "ss", 16)
                    inc(g.dma_start(ed_st[e], ed_d[e]), "ss", 16)

            if debug:
                @block.gpsimd
                def _(g):
                    pre(g, "g", reads=AR)
                    post("sdbg", inc(g.dma_start(dbg["gated"], arena_b[:, :, :]), "sdbg", 16), reads=AR)
            barrier()

        sxn.close()

        with ExitStack() as s2:
            pxT = [s2.enter_context(nc.psum_tensor(f"pxT{i}", [128, 512], F32)) for i in range(2)]
            py = [s2.enter_context(nc.psum_tensor(f"py{i}", [128, 512], F32)) for i in range(2)]
            prow1 = s2.enter_context(nc.psum_tensor("prow1", [128, 512], F32))
            prow2 = s2.enter_context(nc.psum_tensor("prow2", [128, 512], F32))
            pmu = s2.enter_context(nc.psum_tensor("pmu", [128, 512], F32))
            prr = s2.enter_context(nc.psum_tensor("prr", [128, 512], F32))
            x2b = [s2.enter_context(nc.sbuf_tensor(f"x2b{i}", [128, 4, DM], F32)) for i in range(2)]
            xf_sb = s2.enter_context(nc.sbuf_tensor("xf_sb", [128, DC, 512], F32))
            bo_sb = s2.enter_context(nc.sbuf_tensor("bo_sb", [128, DC, 512], F32))
            sq2 = [s2.enter_context(nc.sbuf_tensor(f"sq2{i}", [128, 512], F32)) for i in range(2)]
            tb2 = [s2.enter_context(nc.sbuf_tensor(f"tb2{i}", [128, 512], F32)) for i in range(2)]
            mu1r = s2.enter_context(nc.sbuf_tensor("mu1r", [1, 512], F32))
            r1r = s2.enter_context(nc.sbuf_tensor("r1r", [1, 512], F32))

            for tt in range(NT):
                xs = f"x2b{tt % 2}"

                @block.sync
                def _(y, tt=tt, xs=xs):
                    pre(y, "y", writes=(xs,))
                    src = bass.AP(x_d.tensor, tt * 512 * DM, [[DM, 128], [128 * DM, 4], [1, DM]])
                    sxn_ = f"sx{tt % 2}"
                    post(sxn_, inc(y.dma_start(x2b[tt % 2][:, :, :], src), sxn_, 16), writes=(xs,))

                for dc in range(DC):
                    pxb = f"pxT{dc % 2}"

                    @block.tensor
                    def _(t, tt=tt, dc=dc, xs=xs, pxb=pxb):
                        pre(t, "t", reads=(xs, "identf"), writes=(pxb,))
                        for j in range(4):
                            last = t.transpose(pxT[dc % 2][:, j * 128:(j + 1) * 128],
                                               x2b[tt % 2][:, j, dc * 128:(dc + 1) * 128], identf[:, :])
                        it = inc(last, "ct")
                        post("ct", it, reads=(xs, "identf"), writes=(pxb,))

                    @block.scalar
                    def _(s, tt=tt, dc=dc, pxb=pxb):
                        pre(s, "s", reads=(pxb,), writes=("xf",))
                        ic_ = inc(s.copy(xf_sb[:, dc, :], pxT[dc % 2][:, :]), "cs")
                        post("cs", ic_, reads=(pxb,), writes=("xf",))
                        res["xf"].w["cs"] = ic_

                @block.tensor
                def _(t, tt=tt):
                    pre(t, "t", reads=("xf", "ones128"), writes=("prow1",))
                    for dc in range(DC):
                        last1 = t.matmul(prow1[0:1, :], ones128[:, :], xf_sb[:, dc, :],
                                         start=(dc == 0), stop=(dc == DC - 1))
                    it1 = inc(last1, "ct")
                    post("ct", it1, reads=("xf", "ones128"), writes=("prow1",))

                for dc in range(DC):
                    @block.vector
                    def _(v, tt=tt, dc=dc):
                        pre(v, "v", reads=("xf",), writes=(f"sq2{dc % 2}",))
                        ic_ = inc(v.scalar_tensor_tensor(sq2[dc % 2][:, :], xf_sb[:, dc, :], 0.0,
                                                         xf_sb[:, dc, :], ALU.bypass, ALU.mult), "cv")
                        post("cv", ic_, reads=("xf",), writes=(f"sq2{dc % 2}",))

                    @block.tensor
                    def _(t, tt=tt, dc=dc):
                        pre(t, "t", reads=(f"sq2{dc % 2}",), writes=("prow2",) if dc == 0 else ())
                        last2 = t.matmul(prow2[0:1, :], ones128[:, :], sq2[dc % 2][:, :],
                                         start=(dc == 0), stop=(dc == DC - 1))
                        it2 = inc(last2, "ct")
                        post("ct", it2, reads=(f"sq2{dc % 2}",))
                        res["prow2"].w["ct"] = cnt["ct"]

                def a2_y(dc, tt=tt):
                    pyb = f"py{dc % 2}"

                    @block.tensor
                    def _(t, tt=tt, dc=dc, pyb=pyb):
                        pre(t, "t", reads=("woutsb", AR[tt]), writes=(pyb,))
                        for hcb in range(HCN):
                            last = t.matmul(py[dc % 2][:, :], wout_sb[:, hcb, dc * 128:(dc + 1) * 128],
                                            gated_ap(hcb, tt), start=(hcb == 0), stop=(hcb == HCN - 1))
                        it2 = inc(last, "ct")
                        post("ct", it2, reads=("woutsb", AR[tt]), writes=(pyb,))

                def a2_bo(dc, tt=tt):
                    pyb = f"py{dc % 2}"

                    @block.vector
                    def _(v, tt=tt, dc=dc, pyb=pyb):
                        tbb = f"tb2{dc % 2}"
                        pre(v, "v", reads=("xf", "pmu", "prr", pyb), writes=(tbb, "bo"))
                        iv1 = inc(v.tensor_tensor(tb2[dc % 2][:, :], xf_sb[:, dc, :], pmu[:, :],
                                                  ALU.subtract), "cv")
                        wt(v, "v", [("cv", iv1)])
                        iv2 = inc(v.tensor_tensor(tb2[dc % 2][:, :], tb2[dc % 2][:, :], prr[:, :],
                                                  ALU.mult), "cv")
                        wt(v, "v", [("cv", iv2)])
                        iv3 = inc(v.tensor_tensor(bo_sb[:, dc, :], tb2[dc % 2][:, :],
                                                  py[dc % 2][:, :], ALU.add), "cv")
                        post("cv", iv3, reads=("xf", "pmu", "prr", pyb), writes=(tbb, "bo"))

                a2_y(0)
                a2_y(1)
                row_finalize(prow1[0:1, :], prow2[0:1, :], "prow1", "prow2", DM)

                @block.vector
                def _(v, tt=tt):
                    pre(v, "v", reads=("rows",), writes=("mu1r",))
                    iv = inc(v.tensor_copy(mu1r[:, :], murow[:, :]), "cv")
                    wt(v, "v", [("cv", iv)])
                    iv = inc(v.tensor_copy(r1r[:, :], rrow[:, :]), "cv")
                    post("cv", iv, reads=("rows",), writes=("mu1r",))

                @block.tensor
                def _(t, tt=tt):
                    pre(t, "t", reads=("mu1r", "onesc"), writes=("pmu", "prr"))
                    inc(t.matmul(pmu[:, :], onesc[:, :], mu1r[0:1, :], start=True, stop=True), "ct")
                    it = inc(t.matmul(prr[:, :], onesc[:, :], r1r[0:1, :], start=True, stop=True), "ct")
                    post("ct", it, reads=("mu1r", "onesc"), writes=("pmu", "prr"))

                a2_bo(0)
                a2_bo(1)
                for dc in range(2, DC):
                    a2_y(dc)
                    a2_bo(dc)

                @block.tensor
                def _(t, tt=tt):
                    pre(t, "t", reads=("bo", "ones128"), writes=("prow1",))
                    for dc in range(DC):
                        last1 = t.matmul(prow1[0:1, :], ones128[:, :], bo_sb[:, dc, :],
                                         start=(dc == 0), stop=(dc == DC - 1))
                    it1 = inc(last1, "ct")
                    post("ct", it1, reads=("bo", "ones128"), writes=("prow1",))

                for dc in range(DC):
                    @block.vector
                    def _(v, tt=tt, dc=dc):
                        pre(v, "v", reads=("bo",), writes=(f"sq2{dc % 2}",))
                        ic_ = inc(v.scalar_tensor_tensor(sq2[dc % 2][:, :], bo_sb[:, dc, :], 0.0,
                                                         bo_sb[:, dc, :], ALU.bypass, ALU.mult), "cv")
                        post("cv", ic_, reads=("bo",), writes=(f"sq2{dc % 2}",))

                    @block.tensor
                    def _(t, tt=tt, dc=dc):
                        pre(t, "t", reads=(f"sq2{dc % 2}",), writes=("prow2",) if dc == 0 else ())
                        last2 = t.matmul(prow2[0:1, :], ones128[:, :], sq2[dc % 2][:, :],
                                         start=(dc == 0), stop=(dc == DC - 1))
                        it2 = inc(last2, "ct")
                        post("ct", it2, reads=(f"sq2{dc % 2}",))
                        res["prow2"].w["ct"] = cnt["ct"]

                row_finalize(prow1[0:1, :], prow2[0:1, :], "prow1", "prow2", DM)

                @block.tensor
                def _(t, tt=tt):
                    pre(t, "t", reads=("rows", "onesc"), writes=("pmu", "prr"))
                    inc(t.matmul(pmu[:, :], onesc[:, :], murow[0:1, :], start=True, stop=True), "ct")
                    it = inc(t.matmul(prr[:, :], onesc[:, :], rrow[0:1, :], start=True, stop=True), "ct")
                    post("ct", it, reads=("rows", "onesc"), writes=("pmu", "prr"))

                @block.vector
                def _(v, tt=tt):
                    for dc in range(DC):
                        tbb = f"tb2{dc % 2}"
                        pre(v, "v", reads=("bo", "pmu", "prr", "xf"), writes=(tbb, AR[tt]))
                        iv1 = inc(v.tensor_tensor(tb2[dc % 2][:, :], bo_sb[:, dc, :], pmu[:, :],
                                                  ALU.subtract), "cv")
                        wt(v, "v", [("cv", iv1)])
                        iv2 = inc(v.tensor_tensor(tb2[dc % 2][:, :], tb2[dc % 2][:, :], prr[:, :],
                                                  ALU.mult), "cv")
                        wt(v, "v", [("cv", iv2)])
                        iv3 = inc(v.tensor_tensor(strm_ap(dc, tt), tb2[dc % 2][:, :],
                                                  xf_sb[:, dc, :], ALU.add), "cv")
                        post("cv", iv3, reads=("bo", "pmu", "prr", "xf"), writes=(tbb,))
                        res[AR[tt]].w["cv"] = iv3

            if debug:
                @block.gpsimd
                def _(g):
                    pre(g, "g", reads=AR)
                    post("sdbg", inc(g.dma_start(dbg["strm"], arena[:, :, :]), "sdbg", 16), reads=AR)
            barrier()

        swout.close()

        with ExitStack() as s3:
            pf = [s3.enter_context(nc.psum_tensor(f"pf{i}", [128, 512], F32)) for i in range(8)]
            PF = [f"pf{i}" for i in range(8)]
            ln2b = s3.enter_context(nc.sbuf_tensor("ln2b", [128, DC, SH], BF16))
            routed = s3.enter_context(nc.sbuf_tensor("routed", [128, DC, SH], F32))
            lgf = s3.enter_context(nc.sbuf_tensor("lgf", [NE, SH], F32))
            w_fm = s3.enter_context(nc.sbuf_tensor("w_fm", [NE, SH], F32))
            ln2c = [s3.enter_context(nc.sbuf_tensor(f"ln2c{i}", [128, 512], F32)) for i in range(2)]
            sqf = [s3.enter_context(nc.sbuf_tensor(f"sqf{i}", [128, 512], F32)) for i in range(2)]
            rt_sc = s3.enter_context(nc.sbuf_tensor("rt_sc", [128, 160], F32))
            lgtm = s3.enter_context(nc.sbuf_tensor("lgtm", [128, NE], F32))
            wdtm = s3.enter_context(nc.sbuf_tensor("wdtm", [128, NE], F32))
            sel32 = s3.enter_context(nc.sbuf_tensor("sel32", [128, NE], F32))
            wr32 = s3.enter_context(nc.sbuf_tensor("wr32", [128, NE], F32))

            sco = rt_sc[:, 0:32]
            scb = rt_sc[:, 32:64]
            srt = rt_sc[:, 64:96]
            msk = rt_sc[:, 96:128]
            gsr = rt_sc[:, 128:132]
            u2r = rt_sc[:, 132:134]
            v2r = rt_sc[:, 134:136]
            m2a = rt_sc[:, 136:137]
            m2b = rt_sc[:, 137:138]
            secr = rt_sc[:, 138:139]
            gmr = rt_sc[:, 140:144]
            m8r = rt_sc[:, 144:152]
            denr = rt_sc[:, 152:153]

            for half in range(2):
                hoff = half * NTH

                for tt in range(NTH):
                    gtt = hoff + tt

                    @block.tensor
                    def _(t, tt=tt, gtt=gtt):
                        pre(t, "t", reads=(AR[gtt], "ones128"), writes=("pf0",))
                        for dc in range(DC):
                            last1 = t.matmul(pf[0][0:1, :], ones128[:, :], strm_ap(dc, gtt),
                                             start=(dc == 0), stop=(dc == DC - 1))
                        it1 = inc(last1, "ct")
                        post("ct", it1, reads=(AR[gtt], "ones128"), writes=("pf0",))

                    for dc in range(DC):
                        @block.vector
                        def _(v, tt=tt, gtt=gtt, dc=dc):
                            pre(v, "v", reads=(AR[gtt],), writes=(f"sqf{dc % 2}",))
                            ic_ = inc(v.scalar_tensor_tensor(sqf[dc % 2][:, :], strm_ap(dc, gtt), 0.0,
                                                             strm_ap(dc, gtt), ALU.bypass, ALU.mult), "cv")
                            post("cv", ic_, reads=(AR[gtt],), writes=(f"sqf{dc % 2}",))

                        @block.tensor
                        def _(t, tt=tt, dc=dc):
                            pre(t, "t", reads=(f"sqf{dc % 2}",), writes=("pf1",) if dc == 0 else ())
                            last2 = t.matmul(pf[1][0:1, :], ones128[:, :], sqf[dc % 2][:, :],
                                             start=(dc == 0), stop=(dc == DC - 1))
                            it2 = inc(last2, "ct")
                            post("ct", it2, reads=(f"sqf{dc % 2}",))
                            res["pf1"].w["ct"] = cnt["ct"]

                    row_finalize(pf[0][0:1, :], pf[1][0:1, :], "pf0", "pf1", DM)

                    @block.tensor
                    def _(t, tt=tt):
                        pre(t, "t", reads=("rows", "onesc"), writes=("pf2", "pf3"))
                        inc(t.matmul(pf[2][:, :], onesc[:, :], murow[0:1, :], start=True, stop=True), "ct")
                        it = inc(t.matmul(pf[3][:, :], onesc[:, :], rrow[0:1, :], start=True, stop=True), "ct")
                        post("ct", it, reads=("rows", "onesc"), writes=("pf2", "pf3"))

                    for dc in range(DC):
                        lcb = f"ln2c{dc % 2}"

                        @block.vector
                        def _(v, tt=tt, gtt=gtt, dc=dc, lcb=lcb):
                            pre(v, "v", reads=(AR[gtt], "pf2", "pf3"), writes=(lcb,))
                            iv1 = inc(v.tensor_tensor(ln2c[dc % 2][:, :], strm_ap(dc, gtt), pf[2][:, :],
                                                      ALU.subtract), "cv")
                            wt(v, "v", [("cv", iv1)])
                            iv2 = inc(v.tensor_tensor(ln2c[dc % 2][:, :], ln2c[dc % 2][:, :], pf[3][:, :],
                                                      ALU.mult), "cv")
                            post("cv", iv2, reads=(AR[gtt], "pf2", "pf3"), writes=(lcb,))

                        @block.scalar
                        def _(s, tt=tt, dc=dc, lcb=lcb):
                            pre(s, "s", reads=(lcb,), writes=("ln2b",))
                            ic_ = inc(s.copy(ln2b[:, dc, tt * 512:(tt + 1) * 512], ln2c[dc % 2][:, :]), "cs")
                            post("cs", ic_, reads=(lcb,), writes=("ln2b",))

                        @block.tensor
                        def _(t, tt=tt, dc=dc, lcb=lcb):
                            pre(t, "t", reads=(lcb, "gwT"), writes=("pf4",) if dc == 0 else ())
                            last = t.matmul(pf[4][0:NE, :], gwT[:, dc, :], ln2c[dc % 2][:, :],
                                            start=(dc == 0), stop=(dc == DC - 1))
                            it = inc(last, "ct")
                            post("ct", it, reads=(lcb, "gwT"))
                    res["pf4"].w["ct"] = cnt["ct"]

                    @block.vector
                    def _(v, tt=tt):
                        pre(v, "v", reads=("pf4",), writes=("lgf",))
                        ic_ = inc(v.tensor_copy(lgf[:, tt * 512:(tt + 1) * 512], pf[4][0:NE, :]), "cv")
                        post("cv", ic_, reads=("pf4",), writes=("lgf",))

                for i in range(8):
                    @block.tensor
                    def _(t, half=half, i=i):
                        pre(t, "t", reads=("lgf", "identf"), writes=("pf5",))
                        it = inc(t.transpose(pf[5][:, 0:NE], lgf[:, i * 128:(i + 1) * 128],
                                             identf[0:NE, 0:NE]), "ct")
                        post("ct", it, reads=("lgf", "identf"), writes=("pf5",))

                    @block.vector
                    def _(v, i=i):
                        pre(v, "v", reads=("pf5",), writes=("lgtm",))
                        ic1 = inc(v.tensor_copy(lgtm[:, :], pf[5][:, 0:NE]), "cv")
                        post("cv", ic1, reads=("pf5",), writes=("lgtm",))

                    @block.scalar
                    def _(s, i=i):
                        pre(s, "s", reads=("lgtm",), writes=("rtsc",))
                        ic2 = inc(s.activation(sco[:, :], lgtm[:, :], AF.Sigmoid), "cs")
                        post("cs", ic2, reads=("lgtm",), writes=("rtsc",))

                    @block.vector
                    def _(v, i=i):
                        pre(v, "v", reads=("rtsc", "bias_rep"), writes=("rtsc", "sel32v", "wr32v", "wdtm"))
                        iv = inc(v.tensor_tensor(scb[:, :], sco[:, :], bias_rep[:, :], ALU.add), "cv")
                        wt(v, "v", [("cv", iv)])
                        for g8 in range(4):
                            iv = inc(v.max(out=srt[:, 8 * g8:8 * g8 + 8], in_=scb[:, 8 * g8:8 * g8 + 8]), "cv")
                        wt(v, "v", [("cv", iv)])
                        wt(v, "v", [("cv", iv)])
                        iv = inc(v.tensor_tensor(gsr[:, :], srt[:, 0:32:8], srt[:, 1:32:8], ALU.add), "cv")
                        wt(v, "v", [("cv", iv)])
                        iv = inc(v.tensor_tensor(u2r[:, :], gsr[:, 0:4:2], gsr[:, 1:4:2], ALU.max), "cv")
                        wt(v, "v", [("cv", iv)])
                        iv = inc(v.tensor_tensor(v2r[:, :], gsr[:, 0:4:2], gsr[:, 1:4:2], ALU.min), "cv")
                        wt(v, "v", [("cv", iv)])
                        iv = inc(v.tensor_tensor(m2a[:, :], v2r[:, 0:1], v2r[:, 1:2], ALU.max), "cv")
                        wt(v, "v", [("cv", iv)])
                        iv = inc(v.tensor_tensor(m2b[:, :], u2r[:, 0:1], u2r[:, 1:2], ALU.min), "cv")
                        wt(v, "v", [("cv", iv)])
                        iv = inc(v.tensor_tensor(secr[:, :], m2a[:, :], m2b[:, :], ALU.max), "cv")
                        wt(v, "v", [("cv", iv)])
                        iv = inc(v.tensor_scalar(gmr[:, :], gsr[:, :], secr[:, 0:1], None, ALU.is_ge), "cv")
                        wt(v, "v", [("cv", iv)])
                        iv = inc(v.tensor_tensor(msk[:, :].rearrange("p (g k) -> p g k", k=8),
                                                 scb[:, :].rearrange("p (g k) -> p g k", k=8),
                                                 gmr[:, :].to_broadcast([128, 4, 8]), ALU.mult), "cv")
                        wt(v, "v", [("cv", iv)])
                        iv = inc(v.max(out=m8r[:, :], in_=msk[:, :]), "cv")
                        wt(v, "v", [("cv", iv)])
                        iv = inc(v.tensor_scalar(sel32[:, :], msk[:, :], m8r[:, 7:8], None, ALU.is_ge), "cv")
                        wt(v, "v", [("cv", iv)])
                        iv = inc(v.tensor_tensor(wr32[:, :], sco[:, :], sel32[:, :], ALU.mult), "cv")
                        wt(v, "v", [("cv", iv)])
                        iv = inc(v.reduce_sum(denr[:, :], wr32[:, :], axis=AXX), "cv")
                        wt(v, "v", [("cv", iv)])
                        iv = inc(v.tensor_scalar(denr[:, :], denr[:, :], 1e-20, None, ALU.add), "cv")
                        wt(v, "v", [("cv", iv)])
                        iv = inc(v.reciprocal(denr[:, :], denr[:, :]), "cv")
                        wt(v, "v", [("cv", iv)])
                        iv = inc(v.tensor_scalar(wdtm[:, :], wr32[:, :], denr[:, 0:1], 2.5,
                                                 ALU.mult, ALU.mult), "cv")
                        post("cv", iv, reads=("rtsc", "bias_rep"),
                             writes=("rtsc", "sel32v", "wr32v", "wdtm"))

                    @block.tensor
                    def _(t, i=i):
                        pre(t, "t", reads=("wdtm", "identf"), writes=("pf5",))
                        it = inc(t.transpose(pf[5][0:NE, 0:128], wdtm[:, :], identf[:, :]), "ct")
                        post("ct", it, reads=("wdtm", "identf"), writes=("pf5",))

                    @block.vector
                    def _(v, i=i):
                        pre(v, "v", reads=("pf5",), writes=("wfm",))
                        ic_ = inc(v.tensor_copy(w_fm[:, i * 128:(i + 1) * 128], pf[5][0:NE, 0:128]), "cv")
                        post("cv", ic_, reads=("pf5",), writes=("wfm",))

                if debug:
                    @block.gpsimd
                    def _(g, half=half):
                        pre(g, "g", reads=("lgf", "wfm", "ln2b"))
                        inc(g.dma_start(dbg["lgf"][half], lgf[:, :]), "sdbg", 16)
                        inc(g.dma_start(dbg["wfm"][half], w_fm[:, :]), "sdbg", 16)
                        ig = inc(g.dma_start(dbg["ln2"][half], ln2b[:, :, :]), "sdbg", 16)
                        post("sdbg", ig, reads=("lgf", "wfm", "ln2b"))

                sh_sc = ExitStack()
                shgT = sh_sc.enter_context(nc.sbuf_tensor(f"shgT{half}", [128, DC, 512], BF16))
                shuT = sh_sc.enter_context(nc.sbuf_tensor(f"shuT{half}", [128, DC, 512], BF16))
                shdT = sh_sc.enter_context(nc.sbuf_tensor(f"shdT{half}", [128, 4, DM], BF16))
                mid = sh_sc.enter_context(nc.sbuf_tensor(f"mid{half}", [128, 4, SH], BF16))
                sgl = sh_sc.enter_context(nc.sbuf_tensor(f"sgl{half}", [128, 512], F32))

                for i2h in range(2):
                    @block.sync
                    def _(y, i2h=i2h):
                        wt(y, "y", [("ss2", 48)])
                        pre(y, "y", writes=("shT",))
                        inc(y.dma_start(shgT[:, :, :], shg_st[i2h * 512:(i2h + 1) * 512, :],
                                        transpose=True), "st3", 16)
                        inc(y.dma_start(shuT[:, :, :], shu_st[i2h * 512:(i2h + 1) * 512, :],
                                        transpose=True), "st3", 16)
                        inc(y.dma_start(shdT[:, :, :], shd_st[:, i2h * 512:(i2h + 1) * 512],
                                        transpose=True), "st3", 16)
                        post("st3", cnt["st3"], writes=("shT",))

                    for tt in range(NTH):
                        for ic in range(4):
                            pga, pgan = pf[0 + 2 * (ic % 2)], f"pf{0 + 2 * (ic % 2)}"
                            pua, puan = pf[1 + 2 * (ic % 2)], f"pf{1 + 2 * (ic % 2)}"

                            @block.tensor
                            def _(t, tt=tt, ic=ic, pga=pga, pua=pua, pgan=pgan, puan=puan):
                                pre(t, "t", reads=("shT", "ln2b"), writes=(pgan, puan))
                                for dc in range(DC):
                                    t.matmul(pga[:, :], shgT[:, dc, ic * 128:(ic + 1) * 128],
                                             ln2b[:, dc, tt * 512:(tt + 1) * 512],
                                             start=(dc == 0), stop=(dc == DC - 1))
                                for dc in range(DC):
                                    last = t.matmul(pua[:, :], shuT[:, dc, ic * 128:(ic + 1) * 128],
                                                    ln2b[:, dc, tt * 512:(tt + 1) * 512],
                                                    start=(dc == 0), stop=(dc == DC - 1))
                                it = inc(last, "ct")
                                post("ct", it, reads=("shT", "ln2b"), writes=(pgan, puan))

                            @block.scalar
                            def _(s, tt=tt, ic=ic, pga=pga, pgan=pgan):
                                pre(s, "s", reads=(pgan,), writes=("sgl",))
                                ic_ = inc(s.activation(sgl[:, :], pga[:, :], AF.Silu), "cs")
                                post("cs", ic_, reads=(pgan,), writes=("sgl",))

                            @block.vector
                            def _(v, tt=tt, ic=ic, pua=pua, puan=puan):
                                pre(v, "v", reads=("sgl", puan), writes=("mid",))
                                iv = inc(v.tensor_tensor(mid[:, ic, tt * 512:(tt + 1) * 512], sgl[:, :],
                                                         pua[:, :], ALU.mult), "cv")
                                post("cv", iv, reads=("sgl", puan), writes=("mid",))

                    for tt in range(NTH):
                        for dc in range(DC):
                            pda, pdan = pf[4 + dc % 2], f"pf{4 + dc % 2}"

                            @block.tensor
                            def _(t, tt=tt, dc=dc, pda=pda, pdan=pdan):
                                pre(t, "t", reads=("shT", "mid"), writes=(pdan,))
                                for ic in range(4):
                                    last = t.matmul(pda[:, :], shdT[:, ic, dc * 128:(dc + 1) * 128],
                                                    mid[:, ic, tt * 512:(tt + 1) * 512],
                                                    start=(ic == 0), stop=(ic == 3))
                                it = inc(last, "ct")
                                post("ct", it, reads=("shT", "mid"), writes=(pdan,))

                            if i2h == 0:
                                @block.vector
                                def _(v, tt=tt, dc=dc, pda=pda, pdan=pdan):
                                    pre(v, "v", reads=(pdan,), writes=("routed",))
                                    ic_ = inc(v.tensor_copy(routed[:, dc, tt * 512:(tt + 1) * 512],
                                                            pda[:, :]), "cv")
                                    post("cv", ic_, reads=(pdan,), writes=("routed",))
                            else:
                                @block.vector
                                def _(v, tt=tt, dc=dc, pda=pda, pdan=pdan):
                                    pre(v, "v", reads=(pdan, "routed"), writes=("routed",))
                                    iv = inc(v.tensor_tensor(routed[:, dc, tt * 512:(tt + 1) * 512],
                                                             routed[:, dc, tt * 512:(tt + 1) * 512],
                                                             pda[:, :], ALU.add), "cv")
                                    post("cv", iv, reads=(pdan,), writes=("routed",))

                barrier()
                sh_sc.close()
                ex_sc = ExitStack()
                egT = [ex_sc.enter_context(nc.sbuf_tensor(f"egT{half}_{i}", [128, DC, NI], BF16))
                       for i in range(2)]
                euT = [ex_sc.enter_context(nc.sbuf_tensor(f"euT{half}_{i}", [128, DC, NI], BF16))
                       for i in range(2)]
                edT1 = ex_sc.enter_context(nc.sbuf_tensor(f"edT{half}", [128, 4, DM], BF16))
                act = ex_sc.enter_context(nc.sbuf_tensor(f"act{half}", [128, 4, NTH, 512], BF16))
                wrep_sb = ex_sc.enter_context(nc.sbuf_tensor(f"wrep{half}", [128, NTH, 512], F32))
                sgl4 = ex_sc.enter_context(nc.sbuf_tensor(f"sgl4_{half}", [128, 4, 512], F32))

                for e in range(NE):
                    slot = e % 2
                    ebn = f"eb{slot}"

                    @block.sync
                    def _(y, half=half, e=e, slot=slot, ebn=ebn):
                        wt(y, "y", [("ss", 16 * 96)])
                        pre(y, "y", writes=(ebn,))
                        stn = f"st{slot}"
                        inc(y.dma_start(egT[slot][:, :, :], eg_st[e], transpose=True), stn, 16)
                        inc(y.dma_start(euT[slot][:, :, :], eu_st[e], transpose=True), stn, 16)
                        post(stn, cnt[stn], writes=(ebn,))
                        pre(y, "y", writes=("edt",))
                        inc(y.dma_start(edT1[:, :, :], ed_st[e], transpose=True), "st2", 16)
                        post("st2", cnt["st2"], writes=("edt",))

                    @block.vector
                    def _(v, e=e):
                        pre(v, "v", reads=("iok",), writes=("sele",))
                        iv = inc(v.tensor_scalar(sel_ee[:, :], iok[:, :], float(e), None,
                                                 ALU.is_equal), "cv")
                        post("cv", iv, reads=("iok",), writes=("sele",))

                    @block.tensor
                    def _(t, e=e):
                        pre(t, "t", reads=("sele", "wfm"), writes=("pf4", "pf5"))
                        t.matmul(pf[4][:, :], sel_ee[:, :], w_fm[:, 0:512], start=True, stop=True)
                        it = inc(t.matmul(pf[5][:, :], sel_ee[:, :], w_fm[:, 512:1024],
                                          start=True, stop=True), "ct")
                        post("ct", it, reads=("sele", "wfm"), writes=("pf4", "pf5"))

                    @block.vector
                    def _(v, e=e):
                        pre(v, "v", reads=("pf4", "pf5"), writes=("wrepsb",))
                        v.tensor_copy(wrep_sb[:, 0, :], pf[4][:, :])
                        iv = inc(v.tensor_copy(wrep_sb[:, 1, :], pf[5][:, :]), "cv")
                        post("cv", iv, reads=("pf4", "pf5"), writes=("wrepsb",))

                    for tt in range(NTH):
                        @block.tensor
                        def _(t, e=e, tt=tt, slot=slot, ebn=ebn):
                            for ic in range(4):
                                pre(t, "t", reads=(ebn, "ln2b"), writes=(f"pf{ic}",))
                            for ic in range(4):
                                for dc in range(DC):
                                    last = t.matmul(pf[ic][:, :],
                                                    egT[slot][:, dc, ic * 128:(ic + 1) * 128],
                                                    ln2b[:, dc, tt * 512:(tt + 1) * 512],
                                                    start=(dc == 0), stop=(dc == DC - 1))
                                it = inc(last, "ct")
                                post("ct", it, reads=(ebn, "ln2b"), writes=(f"pf{ic}",))
                            for ic in range(4):
                                pre(t, "t", reads=(ebn, "ln2b", "wrepsb"), writes=(f"pf{4 + ic}",))
                            for ic in range(4):
                                for dc in range(DC):
                                    last = t.matmul(pf[4 + ic][:, :],
                                                    euT[slot][:, dc, ic * 128:(ic + 1) * 128],
                                                    ln2b[:, dc, tt * 512:(tt + 1) * 512],
                                                    start=(dc == 0), stop=(dc == DC - 1))
                                it = inc(last, "ct")
                                post("ct", it, reads=(ebn, "ln2b"), writes=(f"pf{4 + ic}",))

                        @block.scalar
                        def _(s, e=e, tt=tt):
                            for ic in range(4):
                                pre(s, "s", reads=(f"pf{ic}",), writes=(f"sgl4_{ic}",))
                                ic_ = inc(s.activation(sgl4[:, ic, :], pf[ic][:, :], AF.Silu), "cs")
                                post("cs", ic_, reads=(f"pf{ic}",), writes=(f"sgl4_{ic}",))

                        @block.vector
                        def _(v, e=e, tt=tt):
                            for ic in range(4):
                                pre(v, "v", reads=(f"sgl4_{ic}", f"pf{4 + ic}"), writes=(f"sgl4_{ic}",))
                                iv1 = inc(v.tensor_tensor(sgl4[:, ic, :], sgl4[:, ic, :],
                                                          pf[4 + ic][:, :], ALU.mult), "cv")
                                post("cv", iv1, reads=(f"pf{4 + ic}",), writes=(f"sgl4_{ic}",))
                                pre(v, "v", reads=(f"sgl4_{ic}", "wrepsb"), writes=("act",))
                                iv2 = inc(v.tensor_tensor(act[:, ic, tt, :], sgl4[:, ic, :],
                                                          wrep_sb[:, tt, :], ALU.mult), "cv")
                                post("cv", iv2, reads=(f"sgl4_{ic}", "wrepsb"), writes=("act",))

                    for tt in range(NTH):
                        for dc in range(DC):
                            pda = pf[4 + 2 * tt + dc % 2]
                            pdan = f"pf{4 + 2 * tt + dc % 2}"

                            @block.tensor
                            def _(t, e=e, tt=tt, dc=dc, pda=pda, pdan=pdan):
                                pre(t, "t", reads=("edt", "act"), writes=(pdan,))
                                for ic in range(4):
                                    last = t.matmul(pda[:, :], edT1[:, ic, dc * 128:(dc + 1) * 128],
                                                    act[:, ic, tt, :],
                                                    start=(ic == 0), stop=(ic == 3))
                                it = inc(last, "ct")
                                post("ct", it, reads=("edt", "act"), writes=(pdan,))

                            @block.vector
                            def _(v, e=e, tt=tt, dc=dc, pda=pda, pdan=pdan):
                                pre(v, "v", reads=(pdan, "routed"), writes=("routed",))
                                iv = inc(v.tensor_tensor(routed[:, dc, tt * 512:(tt + 1) * 512],
                                                         routed[:, dc, tt * 512:(tt + 1) * 512],
                                                         pda[:, :], ALU.add), "cv")
                                post("cv", iv, reads=(pdan,), writes=("routed",))

                if debug:
                    @block.gpsimd
                    def _(g, half=half):
                        pre(g, "g", reads=("routed",))
                        ig = inc(g.dma_start(dbg["routed"][half], routed[:, :, :]), "sdbg", 16)
                        post("sdbg", ig, reads=("routed",))

                barrier()
                ex_sc.close()
                o_sc = ExitStack()
                out_tm = o_sc.enter_context(nc.sbuf_tensor(f"outtm{half}", [128, 4, DM], F32))

                for tt in range(NTH):
                    gtt = hoff + tt

                    @block.tensor
                    def _(t, tt=tt):
                        pre(t, "t", reads=("routed", "ones128"), writes=("pf0",))
                        for dc in range(DC):
                            last1 = t.matmul(pf[0][0:1, :], ones128[:, :],
                                             routed[:, dc, tt * 512:(tt + 1) * 512],
                                             start=(dc == 0), stop=(dc == DC - 1))
                        it1 = inc(last1, "ct")
                        post("ct", it1, reads=("routed", "ones128"), writes=("pf0",))

                    for dc in range(DC):
                        @block.vector
                        def _(v, tt=tt, dc=dc):
                            pre(v, "v", reads=("routed",), writes=(f"sqf{dc % 2}",))
                            src_ap = routed[:, dc, tt * 512:(tt + 1) * 512]
                            ic_ = inc(v.scalar_tensor_tensor(sqf[dc % 2][:, :], src_ap, 0.0,
                                                             src_ap, ALU.bypass, ALU.mult), "cv")
                            post("cv", ic_, reads=("routed",), writes=(f"sqf{dc % 2}",))

                        @block.tensor
                        def _(t, tt=tt, dc=dc):
                            pre(t, "t", reads=(f"sqf{dc % 2}",), writes=("pf1",) if dc == 0 else ())
                            last2 = t.matmul(pf[1][0:1, :], ones128[:, :], sqf[dc % 2][:, :],
                                             start=(dc == 0), stop=(dc == DC - 1))
                            it2 = inc(last2, "ct")
                            post("ct", it2, reads=(f"sqf{dc % 2}",))
                            res["pf1"].w["ct"] = cnt["ct"]

                    row_finalize(pf[0][0:1, :], pf[1][0:1, :], "pf0", "pf1", DM)

                    @block.tensor
                    def _(t, tt=tt):
                        pre(t, "t", reads=("rows", "onesc"), writes=("pf2", "pf3"))
                        inc(t.matmul(pf[2][:, :], onesc[:, :], murow[0:1, :], start=True, stop=True), "ct")
                        it = inc(t.matmul(pf[3][:, :], onesc[:, :], rrow[0:1, :], start=True, stop=True), "ct")
                        post("ct", it, reads=("rows", "onesc"), writes=("pf2", "pf3"))

                    @block.vector
                    def _(v, tt=tt, gtt=gtt):
                        for dc in range(DC):
                            tbb = f"sqf{dc % 2}"
                            pre(v, "v", reads=("routed", "pf2", "pf3", AR[gtt]), writes=(tbb, "routed"))
                            iv1 = inc(v.tensor_tensor(sqf[dc % 2][:, :],
                                                      routed[:, dc, tt * 512:(tt + 1) * 512],
                                                      pf[2][:, :], ALU.subtract), "cv")
                            wt(v, "v", [("cv", iv1)])
                            iv2 = inc(v.tensor_tensor(sqf[dc % 2][:, :], sqf[dc % 2][:, :], pf[3][:, :],
                                                      ALU.mult), "cv")
                            wt(v, "v", [("cv", iv2)])
                            iv3 = inc(v.tensor_tensor(routed[:, dc, tt * 512:(tt + 1) * 512],
                                                      sqf[dc % 2][:, :], strm_ap(dc, gtt), ALU.add), "cv")
                            post("cv", iv3, reads=("pf2", "pf3", AR[gtt]), writes=(tbb,))
                            res["routed"].w["cv"] = iv3

                    for j in range(4):
                        @block.tensor
                        def _(t, tt=tt, j=j):
                            pre(t, "t", reads=("routed", "identf"), writes=("pf6", "pf7"))
                            for dc in range(DC):
                                bank = pf[6] if dc < 4 else pf[7]
                                last = t.transpose(bank[:, (dc % 4) * 128:(dc % 4 + 1) * 128],
                                                   routed[:, dc, tt * 512 + j * 128:tt * 512 + (j + 1) * 128],
                                                   identf[:, :])
                            it = inc(last, "ct")
                            post("ct", it, reads=("routed", "identf"), writes=("pf6", "pf7"))

                        @block.scalar
                        def _(s, tt=tt, j=j):
                            pre(s, "s", reads=("pf6", "pf7"), writes=("outtm",))
                            inc(s.copy(out_tm[:, j, 0:512], pf[6][:, :]), "cs")
                            ic_ = inc(s.copy(out_tm[:, j, 512:1024], pf[7][:, :]), "cs")
                            post("cs", ic_, reads=("pf6", "pf7"))
                            res["outtm"].w["cs"] = ic_

                    @block.sync
                    def _(y, tt=tt, gtt=gtt):
                        pre(y, "y", reads=("outtm",))
                        dst = bass.AP(out_d.tensor, gtt * 512 * DM, [[DM, 128], [128 * DM, 4], [1, DM]])
                        iy = inc(y.dma_start(dst, out_tm[:, :, :]), "so", 16)
                        post("so", iy, reads=("outtm",))

                barrier()
                o_sc.close()

            @block.sync
            def _(y):
                wt(y, "y", [("so", cnt["so"])])
                if debug:
                    pass

            if debug:
                @block.gpsimd
                def _(g):
                    g.wait_ge(sems["sdbg"], cnt["sdbg"])

    es.close()
    return nc


def kernel(**inputs):
    x = np.asarray(inputs["L0_attn_x"], dtype=np.float32)
    base = {
        "rms": np.ascontiguousarray(np.asarray(inputs["L0_attn_rms_scale"], np.float32).reshape(1, DM)),
        "spl": np.ascontiguousarray(np.asarray(inputs["L0_attn_split_scale"], np.float32).reshape(1, DM)),
        "wk": np.ascontiguousarray(np.asarray(inputs["L0_attn_W_K"], np.float32)),
        "wugg": np.ascontiguousarray(np.asarray(inputs["L0_attn_W_ugg"], np.float32)),
        "wout": np.ascontiguousarray(np.asarray(inputs["L0_attn_W_out"], np.float32)),
        "gw": np.ascontiguousarray(np.asarray(inputs["L0_ffn_gate_w"], np.float32)),
        "gb": np.ascontiguousarray(np.asarray(inputs["L0_ffn_gate_bias"], np.float32).reshape(1, NE)),
        "eg": np.ascontiguousarray(np.asarray(inputs["L0_ffn_expert_gate"], np.float32)),
        "eu": np.ascontiguousarray(np.asarray(inputs["L0_ffn_expert_up"], np.float32)),
        "ed": np.ascontiguousarray(np.asarray(inputs["L0_ffn_expert_down"], np.float32)),
        "shg": np.ascontiguousarray(np.asarray(inputs["L0_ffn_shared_gate"], np.float32)),
        "shu": np.ascontiguousarray(np.asarray(inputs["L0_ffn_shared_up"], np.float32)),
        "shd": np.ascontiguousarray(np.asarray(inputs["L0_ffn_shared_down"], np.float32)),
    }
    in_maps = []
    for c in range(8):
        m = dict(base)
        m["x"] = np.ascontiguousarray(x[c])
        in_maps.append(m)
    nc = build(debug=False)
    res_ = run_bass_kernel_spmd(nc, in_maps, core_ids=list(range(8)))
    return np.stack([r["out"] for r in res_.results], axis=0)
